# version 02
# baseline (speedup 1.0000x reference)
import os as _os
import numpy as np
import ml_dtypes
import concourse.bass as bass
import concourse.mybir as mybir
from concourse.bass_utils import run_bass_kernel_spmd
from concourse.replica_groups import maybe_share_collective_output_space
from contextlib import ExitStack

F32 = mybir.dt.float32
BF16 = mybir.dt.bfloat16
FP8 = mybir.dt.float8e4
AF = mybir.ActivationFunctionType
ALU = mybir.AluOpType
AX = mybir.AxisListType

HID = 2048
FOUT = 3072
NS = FOUT // 512
C = 128
GRP = 4
G4 = [[0, 1, 2, 3], [4, 5, 6, 7]]
G8 = [[0, 1, 2, 3, 4, 5, 6, 7]]

MID_K = lambda h: 2 * h
MID_Q = lambda h: 2 * h + 1
MID_V = lambda pi: 8 + pi
MID_Z = lambda pi: 16 + pi


class Eng:
    def __init__(self, name, stream, step=1):
        self.name, self.stream, self.step = name, stream, step
        self.sem = None
        self.plan = 0
        self.emit = 0
        self.drained_at = 0
        self.waited = {}
        self.fences = set()


def build(T, debug=False):
    nc = bass.Bass("TRN2", target_bir_lowering=False)
    nt = T // C
    ng = T // (C * GRP)
    TC = T // 4

    dI = lambda n, s, dt=F32: nc.dram_tensor(n, s, dt, kind="ExternalInput").ap()
    xa = dI("xa", [T, HID])
    xc = dI("xc", [TC, HID])
    wpT = dI("wpT", [HID, FOUT])
    wbaT = dI("wbaT", [HID, 16])
    woutTs = dI("woutTs", [1024, HID])
    wffnT = dI("wffnT", [HID, HID])
    convwD = dI("convw", [128, 16, 4])
    bavecD = dI("bavec", [8, 1])
    alogD = dI("alog", [8, 1])
    normwD = dI("normw", [128, 1])
    fbiasD = dI("fbias", [1, HID])
    wscalD = dI("wscal", [1, 1])
    identfD = dI("identf", [128, 128])
    identbD = dI("identb", [128, 128], BF16)
    negIbD = dI("negIb", [128, 128], BF16)
    maskpD = dI("maskp", [128, 384], BF16)
    onesfD = dI("onesf", [2, 128])
    onesbD = dI("onesb", [2, 128], BF16)
    hselD = dI("hsel", [128, 4, 8], BF16)
    pselD = dI("psel", [128, 8, 8], BF16)

    out = nc.dram_tensor("out", [TC, HID], F32, kind="ExternalOutput").ap()
    dbg = {}
    if debug:
        for name, shape in [("d_cpack", [2, 8, 384]), ("d_lstrip", [1, 8, 128]),
                            ("d_tmq", [128, 4, 40]), ("d_E", [128, 384]),
                            ("d_w", [128, 256]), ("d_TA", [128, 384]),
                            ("d_qc", [128, 16, T]), ("d_xln", [128, HID]),
                            ("d_rows", [8, 6, T]), ("d_o", [128, 8, T]),
                            ("d_gated", [128, 8, T]), ("d_yres", [TC, HID]),
                            ("d_xln2", [TC, HID]), ("d_ypart", [T, HID])]:
            dbg[name] = nc.dram_tensor(name, shape, F32, kind="ExternalOutput").ap()

    gatedD = nc.dram_tensor("gatedD", [8, 128, T], BF16)
    ypartD = nc.dram_tensor("ypartD", [T, HID], BF16)
    rsoutD = nc.dram_tensor("rsoutD", [TC, HID], BF16,
                            addr_space=maybe_share_collective_output_space(
                                "ReduceScatter", G4))
    stashD = nc.dram_tensor("stashD", [TC, HID], BF16)
    yresD = nc.dram_tensor("yresD", [TC, HID], F32)
    arin = nc.dram_tensor("arin", [1, 1], F32)
    arout = nc.dram_tensor("arout", [1, 1], F32,
                           addr_space=maybe_share_collective_output_space(
                               "AllReduce", G8))

    TE = Eng("TE", "PE")
    DVE = Eng("DVE", "DVE")
    SC = Eng("SC", "ACT")
    GP = Eng("GP", "POOL")
    dG = Eng("dG", "POOL", 16)
    dS = Eng("dS", "SP", 16)
    dC = Eng("dC", "SP", 16)
    ccA = Eng("ccA", "POOL")
    ccB = Eng("ccB", "POOL")
    engs = [TE, DVE, SC, GP, dG, dS, dC, ccA, ccB]
    sched = []

    barrier_state = {"toks": None, "pending": set()}

    def barrier():
        barrier_state["toks"] = [(e, e.plan) for e in engs]
        barrier_state["pending"] = set(e.name for e in engs)

    def op(eng, fn, waits=(), inc=True):
        if eng.name in barrier_state["pending"]:
            barrier_state["pending"].discard(eng.name)
            waits = list(waits) + barrier_state["toks"]
        best = {}
        for w in waits:
            if w is None:
                continue
            pe, pc = w
            if pe.step == 16:
                pc = pe.plan
                if pe is not eng:
                    pe.fences.add(pc)
            if pc > best.get(pe.name, (None, -1))[1]:
                best[pe.name] = (pe, pc)
        ws = list(best.values())
        eng.plan += 1
        tok = (eng, eng.plan)
        sched.append((eng, fn, ws, inc))
        return tok

    bank_w = {}
    bank_r = {}

    def bw(bk):
        return [bank_w.get(bk)] + bank_r.get(bk, [])

    def wrote(bk, tok):
        bank_w[bk] = tok
        bank_r[bk] = []

    def read(bk, tok):
        bank_r.setdefault(bk, []).append(tok)

    def rd(bk):
        return [bank_w.get(bk)]

    with ExitStack() as st:
        E = st.enter_context
        for e in engs:
            e.sem = E(nc.semaphore("s" + e.name))
        identf = E(nc.sbuf_tensor("identf_s", [128, 128], F32))
        identb = E(nc.sbuf_tensor("identb_s", [128, 128], BF16))
        negIb = E(nc.sbuf_tensor("negIb_s", [128, 128], BF16))
        maskp = E(nc.sbuf_tensor("maskp_s", [128, 384], BF16))
        onesf = E(nc.sbuf_tensor("onesf_s", [2, 128], F32))
        onesb = E(nc.sbuf_tensor("onesb_s", [2, 128], BF16))
        hsel = E(nc.sbuf_tensor("hsel_s", [128, 4, 8], BF16))
        psel = E(nc.sbuf_tensor("psel_s", [128, 8, 8], BF16))
        convw = E(nc.sbuf_tensor("convw_s", [128, 16, 4], F32))
        bavec = E(nc.sbuf_tensor("bavec_s", [8, 1], F32))
        nega = E(nc.sbuf_tensor("nega_s", [8, 1], F32))
        alog = E(nc.sbuf_tensor("alog_s", [8, 1], F32))
        normw = E(nc.sbuf_tensor("normw_s", [128, 1], F32))
        zero8 = E(nc.sbuf_tensor("zero8", [8, 1], F32))
        pb = [E(nc.psum_tensor(f"pb{i}", [128, 512], F32)) for i in range(8)]

        t_consts = []
        for dst, src in [(identf, identfD), (onesf, onesfD), (convw, convwD),
                         (bavec, bavecD), (alog, alogD), (normw, normwD),
                         (identb, identbD), (negIb, negIbD), (maskp, maskpD),
                         (onesb, onesbD), (hsel, hselD), (psel, pselD)]:
            t_consts.append(op(dC, lambda h, d=dst, s=src: h.dma_start(out=d[:], in_=s)))
        t_consts.append(op(GP, lambda h: h.memset(zero8[:], 0.0)))
        t_na = op(SC, lambda h: h.activation(nega[:], alog[:], AF.Exp),
                  waits=t_consts)
        t_consts.append(op(DVE, lambda h: h.tensor_scalar(
            out=nega[:], in0=nega[:], scalar1=-1.0, scalar2=None,
            op0=ALU.mult), waits=[t_na]))

        with ExitStack() as stA:
            EA = stA.enter_context
            wp = EA(nc.sbuf_tensor("wp", [128, 16, FOUT], BF16))
            wba = EA(nc.sbuf_tensor("wba", [128, 16, 16], BF16))
            x2s = EA(nc.sbuf_tensor("x2s", [128, HID], BF16))
            sbf = EA(nc.sbuf_tensor("sbf", [8, 256], BF16))
            xfm = EA(nc.sbuf_tensor("xfm", [128, 16, 128], BF16,
                                    is_transpose_dest=True))
            qzt = EA(nc.sbuf_tensor("qzt", [128, FOUT], BF16))
            xraw = qzt[:, 0:HID]
            ring = EA(nc.sbuf_tensor("ring", [128, 24, 16 + GRP * 128], BF16,
                                     is_transpose_dest=True))
            qc = EA(nc.sbuf_tensor("qc", [128, 16, GRP * 128], BF16))
            batm = EA(nc.sbuf_tensor("batm", [128, 16], F32))
            stat = EA(nc.sbuf_tensor("stat", [128, 16], F32))
            rows = EA(nc.sbuf_tensor("rows", [128, GRP * 128], F32))
            r_gp = rows[0:8, :]
            fmq = rows[64:104, :]
            r_rowA = EA(nc.sbuf_tensor("r_rowA8", [8, GRP * 128], F32))
            s8b = EA(nc.sbuf_tensor("s8b", [8, GRP * 128], F32))
            r_g = EA(nc.sbuf_tensor("r_g8", [8, GRP * 128], F32))
            r_beta = EA(nc.sbuf_tensor("r_beta8", [8, GRP * 128], F32))
            r_tmp = EA(nc.sbuf_tensor("r_tmp8", [8, GRP * 128], F32))
            s8a = r_tmp
            r_tmp2 = EA(nc.sbuf_tensor("r_tmp28", [8, GRP * 128], F32))
            r_lnn = EA(nc.sbuf_tensor("r_lnn8", [8, GRP * 128], F32))
            r_lam = EA(nc.sbuf_tensor("r_lam8", [8, GRP * 128], F32))

            bpre = EA(nc.sbuf_tensor("bpre", [8, GRP * 128], F32))
            apre = EA(nc.sbuf_tensor("apre", [8, GRP * 128], F32))
            tmq = EA(nc.sbuf_tensor("tmq", [128, GRP, 40], F32))
            hc3 = EA(nc.sbuf_tensor("hc3", [8, 3, 128], BF16))
            lc3 = EA(nc.sbuf_tensor("lc3", [8, 3, 128], BF16))
            cpack = EA(nc.sbuf_tensor("cpack", [2, 8, 384], BF16))
            lstrip = EA(nc.sbuf_tensor("lstrip", [1, 8, 128], BF16))
            rstrip = EA(nc.sbuf_tensor("rstrip", [1, 8, 128], BF16))
            sE12 = EA(nc.sbuf_tensor("sE12", [128, 256], BF16))
            sE3 = EA(nc.sbuf_tensor("sE3", [128, 128], BF16))
            sTp = EA(nc.sbuf_tensor("sTp", [128, 128], BF16))
            sTpT = EA(nc.sbuf_tensor("sTpT", [128, 128], BF16))
            sAp = EA(nc.sbuf_tensor("sAp", [128, 128], BF16))
            sw = [EA(nc.sbuf_tensor(f"sw{i}", [128, 256], BF16)) for i in range(2)]
            sqst = [EA(nc.sbuf_tensor(f"sqst{i}", [128, 128], BF16)) for i in range(2)]
            sqts = [EA(nc.sbuf_tensor(f"sqts{i}", [128, 128], BF16)) for i in range(2)]
            sKb = EA(nc.sbuf_tensor("sKb", [128, 128], BF16))
            sQt = EA(nc.sbuf_tensor("sQt", [128, 128], BF16))
            sRf = EA(nc.sbuf_tensor("sRf", [128, 128], BF16))
            sMT = EA(nc.sbuf_tensor("sMT", [128, 128], BF16))
            sS32 = EA(nc.sbuf_tensor("sS32", [128, 8, 128], F32))
            sSb = EA(nc.sbuf_tensor("sSb", [128, 8, 128], BF16))
            so = EA(nc.sbuf_tensor("so", [128, 8, 128], BF16))
            so2 = EA(nc.sbuf_tensor("so2", [128, 8, 128], BF16))
            rrow = EA(nc.sbuf_tensor("rrow", [8, 128], F32))
            gtmp = EA(nc.sbuf_tensor("gtmp", [128, 128], BF16))
            gbuf = EA(nc.sbuf_tensor("gbuf", [128, 8, 128], BF16))

            t_wp = []
            for k in range(16):
                t_wp.append(op(dG, lambda h, k=k: h.dma_start(
                    out=wp[:, k, :], in_=wpT[k * 128:(k + 1) * 128, :])))
                t_wp.append(op(dG, lambda h, k=k: h.dma_start(
                    out=wba[:, k, :], in_=wbaT[k * 128:(k + 1) * 128, :])))
            t_s0 = op(GP, lambda h: h.memset(sS32[:], 0.0))
            t_s0b = op(GP, lambda h: h.memset(sSb[:], 0.0))
            t_r0 = op(GP, lambda h: h.memset(ring[:, :, 0:16], 0.0))
            allw = t_consts + t_wp + [t_s0, t_s0b, t_r0]

            st_w = [t_s0b] * 8
            st_r = [t_s0b] * 8
            prev = {"tr1": None, "tr2": None, "mm": None, "evq": None,
                    "xcons": None, "gate_done": [], "rows_done": [],
                    "carry": None, "conv_done": [], "strip_rd": [],
                    "gstore": []}
            gst_all = []

            for g in range(ng):
                tr2_toks, bar_toks = [], []
                for tt in range(GRP):
                    ti = g * GRP + tt
                    t0 = ti * C
                    t_x = op(dG, lambda h, t0=t0: h.dma_start(
                        out=xraw, in_=xa[t0:t0 + 128, :]),
                        waits=[prev["xcons"], prev["tr2"]])
                    t_sq = op(DVE, lambda h: h.scalar_tensor_tensor(
                        out=x2s[:], in0=xraw[:], scalar=1.0, in1=xraw[:],
                        op0=ALU.mult, op1=ALU.mult, accum_out=stat[:, 0:1]),
                        waits=[t_x] + prev["rows_done"])
                    t_sum = op(DVE, lambda h: h.tensor_reduce(
                        stat[:, 1:2], xraw, axis=AX.X, op=ALU.add),
                        waits=[t_x])
                    t_m = op(DVE, lambda h: h.tensor_scalar(
                        out=stat[:, 2:3], in0=stat[:, 1:2], scalar1=1.0 / HID,
                        scalar2=None, op0=ALU.mult), waits=[t_sum])
                    t_mq = op(DVE, lambda h: h.tensor_scalar(
                        out=stat[:, 3:4], in0=stat[:, 0:1], scalar1=1.0 / HID,
                        scalar2=None, op0=ALU.mult), waits=[t_sq])
                    t_m2 = op(DVE, lambda h: h.tensor_tensor(
                        out=stat[:, 4:5], in0=stat[:, 2:3], in1=stat[:, 2:3],
                        op=ALU.mult), waits=[t_m])
                    t_var = op(DVE, lambda h: h.scalar_tensor_tensor(
                        out=stat[:, 5:6], in0=stat[:, 4:5], scalar=-1.0,
                        in1=stat[:, 3:4], op0=ALU.mult, op1=ALU.add),
                        waits=[t_m2, t_mq])
                    t_ve = op(DVE, lambda h: h.tensor_scalar(
                        out=stat[:, 6:7], in0=stat[:, 5:6], scalar1=1e-5,
                        scalar2=None, op0=ALU.add), waits=[t_var])
                    t_rc = op(DVE, lambda h: h.reciprocal(
                        stat[:, 7:8], stat[:, 6:7]), waits=[t_ve])
                    t_r = op(SC, lambda h: h.activation(
                        stat[:, 8:9], stat[:, 7:8], AF.Sqrt), waits=[t_rc])
                    t_ln = op(DVE, lambda h: h.tensor_scalar(
                        out=x2s[:], in0=xraw, scalar1=stat[:, 2:3],
                        scalar2=stat[:, 8:9], op0=ALU.subtract, op1=ALU.mult),
                        waits=[t_r, prev["tr1"]])
                    t_xcons = t_ln
                    t_tr1 = op(dS, lambda h: h.dma_start(
                        out=xfm[:], in_=x2s[:], transpose=True),
                        waits=[t_ln, prev["mm"]])
                    prev["tr1"] = t_tr1
                    prev["xcons"] = t_xcons
                    if debug and ti == 0:
                        op(dG, lambda h: h.dma_start(out=dbg["d_xln"], in_=x2s[:]),
                           waits=[t_ln])
                    pw = []
                    for s in range(NS):
                        pw += bw(s)
                    pw += bw(6)
                    last = None
                    for k in range(16):
                        for s in range(NS):
                            last = op(TE, lambda h, k=k, s=s: h.matmul(
                                pb[s][:], xfm[:, k, :],
                                wp[:, k, s * 512:(s + 1) * 512],
                                start=(k == 0), stop=(k == 15)),
                                waits=(allw + [t_tr1] + pw)
                                if (k == 0 and s == 0) else [])
                        last = op(TE, lambda h, k=k: h.matmul(
                            pb[6][:, 0:16], xfm[:, k, :], wba[:, k, :],
                            start=(k == 0), stop=(k == 15)))
                    t_mm = last
                    prev["mm"] = t_mm
                    for s in range(NS):
                        wrote(s, t_mm)
                    wrote(6, t_mm)
                    ev = []
                    for s in range(NS):
                        if s % 2 == 0:
                            tk = op(SC, lambda h, s=s: h.activation(
                                qzt[:, s * 512:(s + 1) * 512], pb[s][:], AF.Copy),
                                waits=[t_mm, prev["tr2"]])
                        else:
                            tk = op(DVE, lambda h, s=s: h.tensor_copy(
                                qzt[:, s * 512:(s + 1) * 512], pb[s][:]),
                                waits=[t_mm, prev["tr2"]])
                        read(s, tk)
                        ev.append(tk)
                    t_bae = op(DVE, lambda h: h.tensor_copy(
                        batm[:], pb[6][:, 0:16]), waits=[t_mm])
                    read(6, t_bae)
                    t_tr2 = op(dS, lambda h, tt=tt: h.dma_start(
                        out=ring[:, :, 16 + tt * 128:16 + (tt + 1) * 128],
                        in_=qzt[:], transpose=True),
                        waits=ev + [prev["carry"]] + prev["gate_done"])
                    prev["tr2"] = t_tr2
                    tr2_toks.append(t_tr2)
                    t_bat0 = op(TE, lambda h: h.transpose(
                        pb[7][0:8, 0:128], batm[:, 0:8], identf[:]),
                        waits=[t_bae] + bw(7))
                    t_bat = op(TE, lambda h: h.transpose(
                        pb[7][0:8, 128:256], batm[:, 8:16], identf[:]))
                    wrote(7, t_bat)
                    t_bar0 = op(DVE, lambda h, tt=tt: h.tensor_copy(
                        bpre[:, tt * 128:(tt + 1) * 128], pb[7][0:8, 0:128]),
                        waits=[t_bat] + prev["rows_done"])
                    t_bar = op(DVE, lambda h, tt=tt: h.tensor_copy(
                        apre[:, tt * 128:(tt + 1) * 128], pb[7][0:8, 128:256]),
                        waits=[t_bat])
                    read(7, t_bar0)
                    read(7, t_bar)
                    bar_toks += [t_bar0, t_bar]

                W = GRP * 128
                t_rdy = tr2_toks + bar_toks
                conv_t = {}
                for j in range(4):
                    for k in range(16):
                        if j == 0:
                            conv_t[k] = op(DVE, lambda h, k=k: h.tensor_scalar(
                                out=qc[:, k, :], in0=ring[:, k, 13:13 + W],
                                scalar1=convw[:, k, 0:1], scalar2=None,
                                op0=ALU.mult),
                                waits=t_rdy + prev["conv_done"])
                        else:
                            conv_t[k] = op(DVE, lambda h, k=k, j=j: h.scalar_tensor_tensor(
                                out=qc[:, k, :], in0=ring[:, k, 13 + j:13 + j + W],
                                scalar=convw[:, k, j:j + 1], in1=qc[:, k, :],
                                op0=ALU.mult, op1=ALU.add), waits=[conv_t[k]])
                t_conv = [conv_t[k] for k in range(16)]
                t_carry = op(DVE, lambda h: h.tensor_copy(
                    ring[:, :, 13:16], ring[:, :, 13 + W:16 + W]),
                    waits=t_rdy + t_conv)
                prev["carry"] = t_carry
                if debug:
                    op(dG, lambda h, g=g: h.dma_start(
                        out=dbg["d_qc"][:, :, g * W:(g + 1) * W], in_=qc[:]),
                        waits=t_conv)

                t_sg = op(SC, lambda h: h.activation(
                    r_beta[:], bpre[:], AF.Sigmoid), waits=bar_toks)
                t_e1 = op(SC, lambda h: h.activation(
                    r_tmp[:], apre[:], AF.Exp, bias=bavec[:]),
                    waits=bar_toks)
                t_dt = op(SC, lambda h: h.activation(
                    r_tmp[:], r_tmp[:], AF.Ln, bias=1.0), waits=[t_e1])
                t_lg = op(DVE, lambda h: h.tensor_scalar(
                    out=r_tmp2[:], in0=r_tmp[:], scalar1=nega[:], scalar2=None,
                    op0=ALU.mult), waits=[t_dt])
                g_toks = []
                for cc in range(GRP):
                    g_toks.append(op(DVE, lambda h, cc=cc: h.tensor_tensor_scan(
                        out=r_g[:, cc * 128:(cc + 1) * 128],
                        data0=r_tmp2[:, cc * 128:(cc + 1) * 128],
                        data1=r_tmp2[:, cc * 128:(cc + 1) * 128],
                        initial=zero8[:], op0=ALU.add, op1=ALU.bypass),
                        waits=[t_lg]))
                t_gpv = op(DVE, lambda h: h.tensor_tensor(
                    out=r_gp[:], in0=r_g[:], in1=r_tmp2[:], op=ALU.subtract),
                    waits=g_toks)
                t_lb = op(SC, lambda h: h.activation(
                    r_tmp[:], r_beta[:], AF.Ln), waits=[t_sg, t_lg])
                ksq = []
                for hh in range(4):
                    ksq.append(op(SC, lambda h, hh=hh: h.activation(
                        x2s[:].rearrange("p (a b) -> p a b", a=4)[:, hh, 0:W],
                        qc[:, MID_K(hh), :], AF.Square),
                        waits=t_conv + [prev["xcons"], prev["mm"], prev["tr1"]]))
                kn = None
                for hh in range(4):
                    kn = op(TE, lambda h, hh=hh: h.matmul(
                        pb[7][0:8, 0:W], hsel[:, hh, :],
                        x2s[:].rearrange("p (a b) -> p a b", a=4)[:, hh, 0:W],
                        start=(hh == 0), stop=(hh == 3)),
                        waits=[ksq[hh]] + (bw(7) if hh == 0 else []))
                wrote(7, kn)
                t_kn = op(DVE, lambda h: h.tensor_scalar(
                    out=r_lnn[:], in0=pb[7][0:8, 0:W], scalar1=1e-24,
                    scalar2=None, op0=ALU.max), waits=[kn])
                read(7, t_kn)
                t_lnn = op(SC, lambda h: h.activation(
                    r_lnn[:], r_lnn[:], AF.Ln), waits=[t_kn])
                t_ra0 = op(DVE, lambda h: h.tensor_tensor(
                    out=r_rowA[:], in0=r_tmp[:], in1=r_g[:], op=ALU.subtract),
                    waits=[t_lb, t_gpv])
                t_ra = op(DVE, lambda h: h.scalar_tensor_tensor(
                    out=r_rowA[:], in0=r_lnn[:], scalar=-0.5, in1=r_rowA[:],
                    op0=ALU.mult, op1=ALU.add), waits=[t_ra0, t_lnn])
                t_gph = op(DVE, lambda h: h.scalar_tensor_tensor(
                    out=r_gp[:], in0=r_lnn[:], scalar=-0.5, in1=r_gp[:],
                    op0=ALU.mult, op1=ALU.add), waits=[t_gpv, t_lnn])
                t_lam = op(SC, lambda h: h.activation(
                    r_lam[:], r_g[:], AF.Exp), waits=[t_gpv, t_lb])

                t_kt1 = op(SC, lambda h: h.activation(
                    s8a[:], r_gp[:], AF.Exp), waits=[t_gph])
                t_ktr = op(dC, lambda h: h.dma_start(
                    out=rows[64:72, :], in_=s8a[:]), waits=[t_kt1])
                t_gh = op(DVE, lambda h: h.scalar_tensor_tensor(
                    out=r_tmp2[:], in0=r_lnn[:], scalar=0.5, in1=r_g[:],
                    op0=ALU.mult, op1=ALU.add), waits=[t_lnn, t_gpv, t_lg])
                kb_toks, lc_toks = [], []
                for cc in range(GRP):
                    sl = slice(cc * 128, (cc + 1) * 128)
                    tk1 = op(SC, lambda h, sl=sl, cc=cc: h.activation(
                        s8b[:, sl], r_tmp2[:, sl], AF.Exp, scale=-1.0,
                        bias=r_g[:, cc * 128 + 127:cc * 128 + 128]),
                        waits=[t_gh] + g_toks)
                    kb_toks.append(op(DVE, lambda h, sl=sl: h.scalar_tensor_tensor(
                        out=s8b[:, sl], in0=s8b[:, sl], scalar=1.0,
                        in1=r_beta[:, sl], op0=ALU.mult, op1=ALU.mult),
                        waits=[tk1, t_sg]))
                kb_dma = op(dC, lambda h: h.dma_start(
                    out=rows[72:80, :], in_=s8b[:]), waits=kb_toks)
                kb_toks.append(kb_dma)
                for cc in range(GRP):
                    sl = slice(cc * 128, (cc + 1) * 128)
                    lc_toks.append(op(SC, lambda h, sl=sl, cc=cc: h.activation(
                        s8a[:, sl], r_g[:, sl], AF.Exp, scale=0.0,
                        bias=r_g[:, cc * 128 + 127:cc * 128 + 128]),
                        waits=[t_ktr]))
                lc_dma = op(dC, lambda h: h.dma_start(
                    out=rows[96:104, :], in_=s8a[:]), waits=lc_toks)
                lc_toks.append(lc_dma)
                t_cpA = op(dC, lambda h: h.dma_start(
                    out=rows[80:88, :], in_=r_rowA[:]), waits=[t_ra])
                t_cpG = op(dC, lambda h: h.dma_start(
                    out=rows[88:96, :], in_=r_gp[:]), waits=[t_gph])

                tmq_toks = []
                for cc in range(GRP):
                    tpre = ([t_ktr, t_cpA, t_cpG] + kb_toks + lc_toks + bw(7))
                    a = op(TE, lambda h, cc=cc: h.transpose(
                        pb[7][:, 128:168], rows[64:104, cc * 128:(cc + 1) * 128],
                        identf[64:104, 64:104]), waits=tpre)
                    wrote(7, a)
                    b = op(DVE, lambda h, cc=cc: h.tensor_copy(
                        tmq[:, cc, :], pb[7][:, 128:168]), waits=[a])
                    read(7, b)
                    tmq_toks.append(b)
                rows_done = [t_ra, t_cpA, t_cpG, t_gh] + kb_toks + lc_toks + tmq_toks
                if debug:
                    dtk = []
                    for qi, src in enumerate([r_g, r_gp, r_rowA, r_beta, r_lnn, r_lam]):
                        dtk.append(op(dC, lambda h, qi=qi, src=src, g=g: h.dma_start(
                            out=dbg["d_rows"][:, qi, g * W:(g + 1) * W], in_=src[:]),
                            waits=rows_done))
                    rows_done = rows_done + dtk
                zn_toks = []
                for pi in range(8):
                    a = op(SC, lambda h, pi=pi: h.activation(
                        ring[:, MID_Z(pi), 16:16 + W], ring[:, MID_Z(pi), 16:16 + W],
                        AF.Silu), waits=t_rdy)
                    zn_toks.append(op(DVE, lambda h, pi=pi: h.tensor_scalar(
                        out=ring[:, MID_Z(pi), 16:16 + W],
                        in0=ring[:, MID_Z(pi), 16:16 + W],
                        scalar1=normw[:], scalar2=None, op0=ALU.mult), waits=[a]))

                K_NOSCAN = _os.environ.get("K_NOSCAN", "0") == "1"
                K_LVL = int(_os.environ.get("K_LVL", "8"))
                gate_done_all = []
                strip_rd = list(prev["strip_rd"])
                for cc in range(GRP if not K_NOSCAN else 0):
                    csl = slice(cc * 128, (cc + 1) * 128)
                    strips = []
                    hl = []
                    for qi, srcq in enumerate([r_gp, r_g, r_rowA]):
                        a = op(DVE, lambda h, qi=qi, srcq=srcq, csl=csl: h.tensor_copy(
                            hc3[:, qi, :], srcq[:, csl]),
                            waits=[t_ra, t_gh, t_gph] + prev["strip_rd"])
                        b = op(DVE, lambda h, qi=qi, srcq=srcq, csl=csl: h.tensor_tensor(
                            out=apre[:, 0:128], in0=srcq[:, csl], in1=hc3[:, qi, :],
                            op=ALU.subtract), waits=[a])
                        c2 = op(DVE, lambda h, qi=qi: h.tensor_copy(
                            lc3[:, qi, :], apre[:, 0:128]), waits=[b])
                        hl += [a, c2]
                    for hilo, srcb in [(0, hc3), (1, lc3)]:
                        strips.append(op(dC, lambda h, hilo=hilo, srcb=srcb:
                                         h.dma_start(
                            out=cpack[hilo:hilo + 1, :, :].rearrange(
                                "o p (q t) -> o p q t", q=3),
                            in_=srcb[:, :, :]),
                            waits=hl + prev["strip_rd"]))
                    t_lamb = op(DVE, lambda h, csl=csl: h.tensor_copy(
                        sbf[:, 0:128], r_lam[:, csl]),
                        waits=[t_lam] + prev["strip_rd"])
                    strips.append(op(dC, lambda h: h.dma_start(
                        out=lstrip[:], in_=sbf[:, 0:128]),
                        waits=[t_lamb]))
                    allow = strips + tmq_toks + t_conv
                    oe_l, oq_l, rm_l = {}, {}, {}
                    if K_LVL < 2:
                        prev["strip_rd"] = []
                        continue
                    if debug and g == 0 and cc == 0:
                        op(dG, lambda h: h.dma_start(out=dbg["d_cpack"],
                                                     in_=cpack[:]), waits=strips)
                        op(dG, lambda h: h.dma_start(out=dbg["d_lstrip"],
                                                     in_=lstrip[:]), waits=strips)
                        op(dG, lambda h: h.dma_start(out=dbg["d_tmq"],
                                                     in_=tmq[:]), waits=tmq_toks)
                    for hh in range(4):
                        kk = op(TE, lambda h, hh=hh, csl=csl: h.matmul(
                            pb[2][:, 0:256], qc[:, MID_K(hh), csl],
                            qc[:, 2 * hh:2 * hh + 2, csl],
                            start=True, stop=True), waits=allow + bw(2))
                        wrote(2, kk)
                        ktp = op(TE, lambda h, hh=hh, csl=csl: h.transpose(
                            pb[6][:, 128:192].bitcast(BF16), qc[:, MID_K(hh), csl],
                            identb[:]), waits=bw(6) + t_conv)
                        wrote(6, ktp)
                        if K_LVL < 3:
                            continue
                        for p_ in range(2):
                            pi = 2 * hh + p_
                            xm = op(TE, lambda h: h.matmul(
                                pb[3][:, 0:384], negIb[:], maskp[:],
                                start=True, stop=False), waits=bw(3) + allw)
                            xm = op(TE, lambda h, pi=pi: h.matmul(
                                pb[3][:, 0:384], onesb[:], cpack[:, pi, :],
                                start=False, stop=True), waits=strips)
                            strip_rd.append(xm)
                            wrote(3, xm)
                            e12 = op(SC, lambda h, pi=pi, cc=cc: h.activation(
                                sE12[:], pb[3][:, 0:256], AF.Exp,
                                bias=tmq[:, cc, 16 + pi:17 + pi]), waits=[xm])
                            e3 = op(SC, lambda h, pi=pi, cc=cc: h.activation(
                                sE3[:], pb[3][:, 256:384], AF.Exp,
                                bias=tmq[:, cc, 24 + pi:25 + pi]), waits=[xm])
                            read(3, e12)
                            read(3, e3)
                            if debug and g == 0 and cc == 0 and pi == 0:
                                op(dG, lambda h: h.dma_start(
                                    out=dbg["d_E"][:, 0:256], in_=sE12[:]),
                                    waits=[e12])
                                op(dG, lambda h: h.dma_start(
                                    out=dbg["d_E"][:, 256:384], in_=sE3[:]),
                                    waits=[e3])
                            if K_LVL < 4:
                                continue
                            tp = op(DVE, lambda h: h.scalar_tensor_tensor(
                                out=sTp[:], in0=sE12[:, 0:128], scalar=-1.0,
                                in1=pb[2][:, 0:128], op0=ALU.mult, op1=ALU.mult),
                                waits=[e12, kk])
                            ap_ = op(DVE, lambda h: h.tensor_tensor(
                                out=sAp[:], in0=sE12[:, 128:256],
                                in1=pb[2][:, 128:256], op=ALU.mult),
                                waits=[e12, kk])
                            tpt = op(DVE, lambda h: h.scalar_tensor_tensor(
                                out=sTpT[:], in0=sE3[:], scalar=-1.0,
                                in1=pb[2][:, 0:128], op0=ALU.mult, op1=ALU.mult),
                                waits=[e3, kk])
                            for tkx in (tp, ap_, tpt):
                                read(2, tkx)
                            vt = op(TE, lambda h, pi=pi, csl=csl: h.transpose(
                                pb[6][:, 256:320].bitcast(BF16), qc[:, MID_V(pi), csl],
                                identb[:]), waits=bw(6) + t_conv)
                            wrote(6, vt)
                            uv = op(DVE, lambda h: h.tensor_copy(
                                sw[0][:, 0:128], pb[6][:, 256:320].bitcast(BF16)),
                                waits=[vt])
                            read(6, uv)
                            ukt = op(DVE, lambda h, pi=pi, cc=cc: h.tensor_scalar(
                                out=sw[0][:, 128:256],
                                in0=pb[6][:, 128:192].bitcast(BF16),
                                scalar1=tmq[:, cc, pi:pi + 1], scalar2=None,
                                op0=ALU.mult), waits=[ktp] + tmq_toks + rd(6))
                            read(6, ukt)
                            kbar = op(DVE, lambda h, pi=pi, cc=cc: h.tensor_scalar(
                                out=sKb[:], in0=pb[6][:, 128:192].bitcast(BF16),
                                scalar1=tmq[:, cc, 8 + pi:9 + pi], scalar2=None,
                                op0=ALU.mult), waits=[ktp] + rd(6))
                            read(6, kbar)
                            if debug and g == 0 and cc == 0 and pi == 0:
                                op(dG, lambda h: h.dma_start(
                                    out=dbg["d_TA"][:, 0:128], in_=sTp[:]),
                                    waits=[tp])
                                op(dG, lambda h: h.dma_start(
                                    out=dbg["d_TA"][:, 128:256], in_=sAp[:]),
                                    waits=[ap_])
                                op(dG, lambda h: h.dma_start(
                                    out=dbg["d_TA"][:, 256:384], in_=sTpT[:]),
                                    waits=[tpt])
                            if K_LVL < 5:
                                continue
                            wcur = [uv, ukt]
                            pst, pts = sTp, sTpT
                            ptok = [tp, tpt]
                            wbuf = 0
                            for j in range(7):
                                am = op(TE, lambda h, pst=pst, wb=wbuf: h.matmul(
                                    pb[4][:, 0:256], pst[:], sw[wb][:],
                                    start=True, stop=True),
                                    waits=wcur + ptok + bw(4))
                                wrote(4, am)
                                wn = op(DVE, lambda h, wb=wbuf: h.tensor_tensor(
                                    out=sw[1 - wb][:], in0=pb[4][:, 0:256],
                                    in1=sw[wb][:], op=ALU.add), waits=[am])
                                read(4, wn)
                                wcur = [wn]
                                wbuf = 1 - wbuf
                                if j < 6:
                                    q1 = op(TE, lambda h, pst=pst, pts=pts: h.matmul(
                                        pb[5][:, 0:128], pst[:], pts[:],
                                        start=True, stop=True),
                                        waits=ptok + bw(5))
                                    q2 = op(TE, lambda h, pst=pst, pts=pts: h.matmul(
                                        pb[5][:, 128:256], pts[:], pst[:],
                                        start=True, stop=True))
                                    wrote(5, q2)
                                    nts = op(SC, lambda h, j=j: h.activation(
                                        sqts[j % 2][:], pb[5][:, 0:128], AF.Copy),
                                        waits=[q2])
                                    nst = op(SC, lambda h, j=j: h.activation(
                                        sqst[j % 2][:], pb[5][:, 128:256], AF.Copy),
                                        waits=[q2])
                                    read(5, nts)
                                    read(5, nst)
                                    pst, pts = sqst[j % 2], sqts[j % 2]
                                    ptok = [nst, nts]
                            wfin = wcur[0]
                            wf = sw[wbuf]
                            if debug and g == 0 and cc == 0 and pi == 0:
                                op(dG, lambda h, wf=wf: h.dma_start(
                                    out=dbg["d_w"], in_=wf[:]), waits=[wfin])
                            if K_LVL < 6:
                                continue
                            K_L6 = int(_os.environ.get("K_L6", "9"))
                            m1 = op(TE, lambda h, wf=wf: h.matmul(
                                pb[6][:, 384:512], wf[:, 128:256], sAp[:],
                                start=True, stop=True),
                                waits=[wfin, ap_] + bw(6))
                            wrote(6, m1)
                            if K_L6 < 2:
                                continue
                            lt = op(TE, lambda h, pi=pi: h.matmul(
                                pb[2][:, 256:384], onesb[0:1, :],
                                lstrip[:, pi, :], start=True, stop=True),
                                waits=strips + bw(2))
                            wrote(2, lt)
                            strip_rd.append(lt)
                            if K_L6 < 3:
                                continue
                            qt = op(DVE, lambda h, hh=hh, csl=csl: h.tensor_tensor(
                                out=sQt[:], in0=qc[:, MID_Q(hh), csl],
                                in1=pb[2][:, 256:384], op=ALU.mult),
                                waits=[lt] + t_conv)
                            read(2, qt)
                            rf = op(DVE, lambda h: h.tensor_tensor(
                                out=sRf[:], in0=sQt[:], in1=pb[6][:, 384:512],
                                op=ALU.subtract), waits=[qt, m1])
                            read(6, rf)
                            if K_L6 < 4:
                                continue
                            o1 = op(TE, lambda h, pi=pi: h.matmul(
                                pb[7][:, 168:296], sSb[:, pi, :], sRf[:],
                                start=True, stop=False),
                                waits=[rf, st_w[pi]] + bw(7))
                            o2m = op(TE, lambda h, wf=wf: h.matmul(
                                pb[7][:, 168:296], wf[:, 0:128], sAp[:],
                                start=False, stop=True))
                            wrote(7, o2m)
                            st_r[pi] = o1
                            if K_L6 < 5:
                                continue
                            oe = op(DVE, lambda h, pi=pi: h.tensor_copy(
                                so[:, pi, :], pb[7][:, 168:296]),
                                waits=[o2m] + prev["gate_done"])
                            oq = op(SC, lambda h, pi=pi: h.activation(
                                so2[:, pi, :], so[:, pi, :], AF.Square),
                                waits=[oe])
                            read(7, oe)
                            read(7, oq)
                            oe_l[pi], oq_l[pi] = oe, oq
                            if K_LVL < 7:
                                continue
                            mt = op(TE, lambda h, wf=wf: h.matmul(
                                pb[4][:, 256:384], wf[:, 128:256], sKb[:],
                                start=True, stop=True),
                                waits=[wfin, kbar] + bw(4))
                            wrote(4, mt)
                            mte = op(DVE, lambda h: h.tensor_scalar(
                                out=sMT[:], in0=pb[4][:, 256:384], scalar1=-1.0,
                                scalar2=None, op0=ALU.mult), waits=[mt])
                            read(4, mte)
                            s1 = op(TE, lambda h, pi=pi: h.matmul(
                                pb[5][:, 256:384], sMT[:], sSb[:, pi, :],
                                start=True, stop=False),
                                waits=[mte, st_w[pi], st_r[pi]] + bw(5))
                            s2 = op(TE, lambda h, wf=wf: h.matmul(
                                pb[5][:, 256:384], sKb[:], wf[:, 0:128],
                                start=False, stop=True))
                            wrote(5, s2)
                            su = op(DVE, lambda h, pi=pi, cc=cc: h.scalar_tensor_tensor(
                                out=sS32[:, pi, :], in0=sS32[:, pi, :],
                                scalar=tmq[:, cc, 32 + pi:33 + pi],
                                in1=pb[5][:, 256:384], op0=ALU.mult, op1=ALU.add),
                                waits=[s2, st_r[pi]])
                            read(5, su)
                            sb2 = op(DVE, lambda h, pi=pi: h.tensor_copy(
                                sSb[:, pi, :], sS32[:, pi, :]),
                                waits=[su, st_r[pi]])
                            st_w[pi] = sb2
                    if K_LVL < 8:
                        prev["strip_rd"] = list(strip_rd[-20:])
                        continue
                    rm = None
                    for pi in range(8):
                        rm = op(TE, lambda h, pi=pi: h.matmul(
                            pb[3][0:8, 384:512], psel[:, pi, :], so2[:, pi, :],
                            start=(pi == 0), stop=(pi == 7)),
                            waits=[oq_l[pi]] + (bw(3) if pi == 0 else []))
                    wrote(3, rm)
                    rr = op(DVE, lambda h: h.tensor_scalar(
                        out=rrow[:], in0=pb[3][0:8, 384:512], scalar1=1.0 / 128,
                        scalar2=1e-6, op0=ALU.mult, op1=ALU.add), waits=[rm])
                    read(3, rr)
                    rr2 = op(DVE, lambda h: h.reciprocal(rrow[:], rrow[:]),
                             waits=[rr])
                    rr3 = op(SC, lambda h: h.activation(rrow[:], rrow[:], AF.Sqrt),
                             waits=[rr2])
                    rrb = op(DVE, lambda h: h.tensor_copy(sbf[:, 128:256], rrow[:]),
                             waits=[rr3])
                    rsd = op(dC, lambda h: h.dma_start(
                        out=rstrip[:], in_=sbf[:, 128:256]), waits=[rrb])
                    gate_done = []
                    last_g2 = None
                    for pi in range(8):
                        gb = op(TE, lambda h, pi=pi: h.matmul(
                            pb[2][:, 384:512], onesb[0:1, :], rstrip[:, pi, :],
                            start=True, stop=True), waits=[rsd] + bw(2))
                        wrote(2, gb)
                        g1 = op(DVE, lambda h, pi=pi: h.tensor_tensor(
                            out=gtmp[:], in0=so[:, pi, :], in1=pb[2][:, 384:512],
                            op=ALU.mult),
                            waits=[gb, oe_l[pi]] + ([last_g2] if last_g2 else []))
                        read(2, g1)
                        g2 = op(DVE, lambda h, pi=pi, cc=cc: h.tensor_tensor(
                            out=gbuf[:, pi, :], in0=gtmp[:],
                            in1=ring[:, MID_Z(pi), 16 + cc * 128:16 + (cc + 1) * 128],
                            op=ALU.mult),
                            waits=[g1] + zn_toks + prev["gstore"])
                        last_g2 = g2
                        gate_done.append(g2)
                    if debug:
                        for pi in range(8):
                            op(dG, lambda h, pi=pi, g=g, cc=cc: h.dma_start(
                                out=dbg["d_o"][:, pi,
                                               g * W + cc * 128:g * W + (cc + 1) * 128],
                                in_=so[:, pi, :]), waits=gate_done)
                    gstc = []
                    for pi in range(8):
                        gstc.append(op(dC, lambda h, pi=pi, g=g, cc=cc: h.dma_start(
                            out=gatedD.ap()[pi, :, g * W + cc * 128:
                                            g * W + (cc + 1) * 128],
                            in_=gbuf[:, pi, :]), waits=gate_done))
                    prev["gstore"] = gstc
                    gst_all += gstc
                    prev["gate_done"] = gate_done
                    gate_done_all += gate_done
                    prev["strip_rd"] = list(strip_rd[-20:])
                prev["rows_done"] = rows_done + strip_rd[-8:]
                prev["conv_done"] = strip_rd[-8:] + gate_done_all[-8:] + [t_carry]

        K_PHASES = _os.environ.get("K_PHASES", "ABC")
        if K_PHASES == "A":
            barrier()
            fin = op(dC, lambda h: h.dma_start(out=out[0:128, 0:256],
                                               in_=rows[:, 0:256]), waits=[])
            op(GP, lambda h: h.nop(), waits=[fin] + gst_all, inc=False)
            per_stream = {}
            for item in sched:
                per_stream.setdefault(item[0].stream, []).append(item)
            stream_eng = {"PE": "tensor", "DVE": "vector", "ACT": "scalar",
                          "POOL": "gpsimd", "SP": "sync"}

            def make_body(sname):
                items = per_stream.get(sname, [])

                def body(h):
                    for eng, fn, ws, inc in items:
                        if eng.step == 16 and inc:
                            need = eng.emit in eng.fences
                            if need or eng.emit - eng.waited.get(eng.name, 0) >= 24:
                                tgt = eng.emit
                                if eng.waited.get(eng.name, 0) < tgt:
                                    h.wait_ge(eng.sem, tgt * 16)
                                    eng.waited[eng.name] = tgt
                            if eng.name == "dS":
                                for oth in (dC, dS):
                                    if eng.waited.get("x" + oth.name, 0) < oth.emit:
                                        h.wait_ge(oth.sem, oth.emit * 16)
                                        eng.waited["x" + oth.name] = oth.emit
                            elif eng.name == "dC":
                                if eng.waited.get("xdS", 0) < dS.emit:
                                    h.wait_ge(dS.sem, dS.emit * 16)
                                    eng.waited["xdS"] = dS.emit
                        for w in ws:
                            pe, pc = w
                            if pc <= 0:
                                continue
                            if pe is eng and pe.step != 16:
                                if eng.drained_at < pc:
                                    h.drain()
                                    eng.drained_at = eng.emit
                            else:
                                if eng.waited.get(pe.name, 0) < pc:
                                    h.wait_ge(pe.sem, pc * pe.step)
                                    eng.waited[pe.name] = pc
                        instr = fn(h)
                        if inc:
                            eng.emit += 1
                            instr.then_inc(eng.sem, eng.step)
                return body

            block = E(nc.Block())
            for sname, attr in stream_eng.items():
                getattr(block, attr)(make_body(sname))
            return nc
        barrier()
        with ExitStack() as stB:
            EB = stB.enter_context
            wouts = EB(nc.sbuf_tensor("wouts", [128, 8, HID], BF16))
            gblk = EB(nc.sbuf_tensor("gblk", [128, 8, 512], BF16))
            yp = EB(nc.sbuf_tensor("yp", [128, HID], BF16))
            t_wo = []
            for k in range(8):
                t_wo.append(op(dG, lambda h, k=k: h.dma_start(
                    out=wouts[:, k, :], in_=woutTs[k * 128:(k + 1) * 128, :])))
            ypst = []
            prev_ev = []
            prev_st = []
            for gg in range(nt // GRP):
                gl = []
                for pi in range(8):
                    gl.append(op(dC, lambda h, pi=pi, gg=gg: h.dma_start(
                        out=gblk[:, pi, :],
                        in_=gatedD.ap()[pi, :, gg * 512:(gg + 1) * 512]),
                        waits=gst_all + prev_ev))
                for tb in range(GRP):
                    mm = None
                    for k in range(8):
                        for s in range(4):
                            mm = op(TE, lambda h, k=k, s=s, tb=tb: h.matmul(
                                pb[s][:], gblk[:, k, tb * 128:(tb + 1) * 128],
                                wouts[:, k, s * 512:(s + 1) * 512],
                                start=(k == 0), stop=(k == 7)),
                                waits=(gl + t_wo + bw(0) + bw(1) + bw(2) + bw(3))
                                if (k == 0 and s == 0) else [])
                    for s in range(4):
                        wrote(s, mm)
                    ev = []
                    for s in range(4):
                        if s % 2 == 0:
                            tk = op(SC, lambda h, s=s: h.activation(
                                yp[:, s * 512:(s + 1) * 512], pb[s][:], AF.Copy),
                                waits=[mm] + prev_st)
                        else:
                            tk = op(DVE, lambda h, s=s: h.tensor_copy(
                                yp[:, s * 512:(s + 1) * 512], pb[s][:]),
                                waits=[mm] + prev_st)
                        read(s, tk)
                        ev.append(tk)
                    prev_ev = ev
                    tb_g = gg * GRP + tb
                    t_st = op(dC, lambda h, tb_g=tb_g: h.dma_start(
                        out=ypartD.ap()[tb_g * 128:(tb_g + 1) * 128, :],
                        in_=yp[:]), waits=ev)
                    ypst.append(t_st)
                    prev_st = [t_st]
                    if debug:
                        op(dG, lambda h, tb_g=tb_g: h.dma_start(
                            out=dbg["d_ypart"][tb_g * 128:(tb_g + 1) * 128, :],
                            in_=yp[:]), waits=ev)
            t_rs = op(ccA, lambda h: h.collective_compute(
                "ReduceScatter", ALU.add, replica_groups=G4,
                ins=[ypartD.ap().opt()], outs=[rsoutD.ap().opt()]),
                waits=ypst)

        barrier()
        with ExitStack() as stC:
            EC = stC.enter_context
            rst = EC(nc.sbuf_tensor("rst", [128, HID], BF16))
            xct = EC(nc.sbuf_tensor("xct", [128, HID], F32))
            yrt = EC(nc.sbuf_tensor("yrt", [128, HID], F32))
            xl2 = EC(nc.sbuf_tensor("xl2", [128, HID], F32))
            xl2b = EC(nc.sbuf_tensor("xl2b", [128, HID], BF16))
            statc = EC(nc.sbuf_tensor("statc", [128, 16], F32))
            mxv = EC(nc.sbuf_tensor("mxv", [128, 2], F32))
            sc1 = EC(nc.sbuf_tensor("sc1", [1, 8], F32))
            vscale = EC(nc.sbuf_tensor("vscale", [128, 2], F32))
            wff = EC(nc.sbuf_tensor("wff", [128, 16, HID], BF16))
            xq8 = EC(nc.sbuf_tensor("xq8", [128, 16, 128], FP8))
            xl2r = EC(nc.sbuf_tensor("xl2r", [128, HID], BF16))
            xfmC = EC(nc.sbuf_tensor("xfmC", [128, 16, 128], BF16,
                                     is_transpose_dest=True))
            fbs = EC(nc.sbuf_tensor("fbs", [1, HID], F32))
            bbc = EC(nc.sbuf_tensor("bbc", [128, HID], F32))
            ftmp = EC(nc.sbuf_tensor("ftmp", [128, HID], F32))
            fyb = EC(nc.sbuf_tensor("fyb", [128, HID], F32))

            t_mx0 = op(GP, lambda h: h.memset(mxv[:], 0.0))
            ln2_last = None
            sq_last = None
            pst = []
            for tb in range(TC // 128):
                t_rl = op(dC, lambda h, tb=tb: h.dma_start(
                    out=rst[:], in_=rsoutD.ap()[tb * 128:(tb + 1) * 128, :]),
                    waits=[t_rs] + pst + ([ln2_last] if ln2_last else []))
                t_xl = op(dC, lambda h, tb=tb: h.dma_start(
                    out=xct[:], in_=xc[tb * 128:(tb + 1) * 128, :]),
                    waits=pst)
                t_yr = op(DVE, lambda h: h.tensor_tensor(
                    out=yrt[:], in0=xct[:], in1=rst[:], op=ALU.add),
                    waits=[t_rl, t_xl] + ([ln2_last] if ln2_last else []))
                t_ys = op(dC, lambda h, tb=tb: h.dma_start(
                    out=yresD.ap()[tb * 128:(tb + 1) * 128, :], in_=yrt[:]),
                    waits=[t_yr])
                if debug:
                    op(dC, lambda h, tb=tb: h.dma_start(
                        out=dbg["d_yres"][tb * 128:(tb + 1) * 128, :], in_=yrt[:]),
                        waits=[t_yr])
                t_sq = op(SC, lambda h: h.activation(
                    xl2[:], yrt[:], AF.Square, accum_out=statc[:, 0:1]),
                    waits=[t_yr])
                t_sum = op(DVE, lambda h: h.tensor_reduce(
                    statc[:, 1:2], yrt[:], axis=AX.X, op=ALU.add), waits=[t_yr])
                t_m = op(DVE, lambda h: h.tensor_scalar(
                    out=statc[:, 2:3], in0=statc[:, 1:2], scalar1=1.0 / HID,
                    scalar2=None, op0=ALU.mult), waits=[t_sum])
                t_mq = op(DVE, lambda h: h.tensor_scalar(
                    out=statc[:, 3:4], in0=statc[:, 0:1], scalar1=1.0 / HID,
                    scalar2=None, op0=ALU.mult), waits=[t_sq])
                t_m2 = op(DVE, lambda h: h.tensor_tensor(
                    out=statc[:, 4:5], in0=statc[:, 2:3], in1=statc[:, 2:3],
                    op=ALU.mult), waits=[t_m])
                t_var = op(DVE, lambda h: h.scalar_tensor_tensor(
                    out=statc[:, 5:6], in0=statc[:, 4:5], scalar=-1.0,
                    in1=statc[:, 3:4], op0=ALU.mult, op1=ALU.add),
                    waits=[t_m2, t_mq])
                t_ve = op(DVE, lambda h: h.tensor_scalar(
                    out=statc[:, 6:7], in0=statc[:, 5:6], scalar1=1e-5,
                    scalar2=None, op0=ALU.add), waits=[t_var])
                t_rc = op(DVE, lambda h: h.reciprocal(
                    statc[:, 7:8], statc[:, 6:7]), waits=[t_ve])
                t_r = op(SC, lambda h: h.activation(
                    statc[:, 8:9], statc[:, 7:8], AF.Sqrt), waits=[t_rc])
                t_l2 = op(DVE, lambda h: h.tensor_scalar(
                    out=xl2[:], in0=yrt[:], scalar1=statc[:, 2:3],
                    scalar2=statc[:, 8:9], op0=ALU.subtract, op1=ALU.mult),
                    waits=[t_r, t_sq])
                sq_last = t_l2
                t_l2b = op(DVE, lambda h: h.tensor_copy(xl2b[:], xl2[:]),
                           waits=[t_l2])
                t_stt = op(dC, lambda h, tb=tb: h.dma_start(
                    out=stashD.ap()[tb * 128:(tb + 1) * 128, :], in_=xl2b[:]),
                    waits=[t_l2b])
                if debug:
                    op(dC, lambda h, tb=tb: h.dma_start(
                        out=dbg["d_xln2"][tb * 128:(tb + 1) * 128, :], in_=xl2[:]),
                        waits=[t_l2])
                t_mr = op(DVE, lambda h: h.reduce_max(
                    mxv[:, 1:2], xl2[:], axis=AX.X, apply_absolute_value=True),
                    waits=[t_l2, t_mx0])
                t_mu = op(DVE, lambda h: h.tensor_tensor(
                    out=mxv[:, 0:1], in0=mxv[:, 0:1], in1=mxv[:, 1:2],
                    op=ALU.max), waits=[t_mr])
                ln2_last = t_mu
                pst = [t_ys, t_stt]
            t_mt = op(TE, lambda h: h.transpose(
                pb[2][0:2, 0:128], mxv[:, 0:2], identf[:]),
                waits=[ln2_last] + bw(2))
            wrote(2, t_mt)
            t_mv = op(DVE, lambda h: h.reduce_max(
                sc1[0:1, 0:1], pb[2][0:1, 0:128], axis=AX.X), waits=[t_mt])
            read(2, t_mv)
            t_as = op(dC, lambda h: h.dma_start(out=arin.ap(), in_=sc1[0:1, 0:1]),
                      waits=[t_mv])
            t_ar = op(ccB, lambda h: h.collective_compute(
                "AllReduce", ALU.max, replica_groups=G8,
                ins=[arin.ap().opt()], outs=[arout.ap().opt()]), waits=[t_as])
            t_al = op(dC, lambda h: h.dma_start(out=sc1[0:1, 1:2], in_=arout.ap()),
                      waits=[t_ar])
            t_s1 = op(DVE, lambda h: h.tensor_scalar(
                out=sc1[0:1, 2:3], in0=sc1[0:1, 1:2], scalar1=1.0 / 448.0,
                scalar2=1e-7, op0=ALU.mult, op1=ALU.max), waits=[t_al])
            t_s2 = op(DVE, lambda h: h.reciprocal(sc1[0:1, 3:4], sc1[0:1, 2:3]),
                      waits=[t_s1])
            t_s3 = op(DVE, lambda h: h.tensor_scalar(
                out=sc1[0:1, 4:5], in0=sc1[0:1, 3:4], scalar1=0.5, scalar2=None,
                op0=ALU.mult), waits=[t_s2])
            t_wsl = op(dC, lambda h: h.dma_start(out=sc1[0:1, 6:7], in_=wscalD))
            t_s4 = op(DVE, lambda h: h.tensor_scalar(
                out=sc1[0:1, 5:6], in0=sc1[0:1, 6:7], scalar1=2.0, scalar2=None,
                op0=ALU.mult), waits=[t_wsl])
            t_b1 = op(TE, lambda h: h.matmul(
                pb[2][:, 128:130], onesf[0:1, :], sc1[0:1, 4:6],
                start=True, stop=True), waits=[t_s3, t_s4] + bw(2))
            wrote(2, t_b1)
            t_b2 = op(DVE, lambda h: h.tensor_copy(vscale[:], pb[2][:, 128:130]),
                      waits=[t_b1])
            read(2, t_b2)
            t_wf = []
            for k in range(16):
                t_wf.append(op(dG, lambda h, k=k: h.dma_start(
                    out=wff[:, k, :], in_=wffnT[k * 128:(k + 1) * 128, :])))
            t_fb = op(dC, lambda h: h.dma_start(out=fbs[:], in_=fbiasD))
            bb = []
            for s in range(4):
                a = op(TE, lambda h, s=s: h.matmul(
                    pb[3][:, 0:512], onesf[0:1, :], fbs[0:1, s * 512:(s + 1) * 512],
                    start=True, stop=True), waits=[t_fb] + bw(3))
                wrote(3, a)
                b = op(DVE, lambda h, s=s: h.tensor_copy(
                    bbc[:, s * 512:(s + 1) * 512], pb[3][:, 0:512]), waits=[a])
                read(3, b)
                bb.append(b)
            fprev = []
            final_st = []
            yr_last = sq_last
            for tb in range(TC // 128):
                t_sl = op(dC, lambda h, tb=tb: h.dma_start(
                    out=xl2r[:], in_=stashD.ap()[tb * 128:(tb + 1) * 128, :]),
                    waits=[t_b2] + fprev + pst)
                t_t3 = op(dS, lambda h: h.dma_start(
                    out=xfmC[:], in_=xl2r[:], transpose=True),
                    waits=[t_sl] + fprev)
                t_q = op(DVE, lambda h: h.tensor_scalar(
                    out=xq8[:], in0=xfmC[:], scalar1=vscale[:, 0:1],
                    scalar2=None, op0=ALU.mult), waits=[t_t3, t_b2] + fprev)
                t_yl = op(dC, lambda h, tb=tb: h.dma_start(
                    out=yrt[:], in_=yresD.ap()[tb * 128:(tb + 1) * 128, :]),
                    waits=fprev + [yr_last, t_mu])
                t_yb = op(DVE, lambda h: h.tensor_tensor(
                    out=fyb[:], in0=yrt[:], in1=bbc[:], op=ALU.add),
                    waits=[t_yl] + bb + fprev)
                evl = []
                for s in range(4):
                    mm = None
                    for k in range(16):
                        mm = op(TE, lambda h, s=s, k=k: h.matmul(
                            pb[4 + (s % 2)][:], xq8[:, k, :],
                            wff[:, k, s * 512:(s + 1) * 512],
                            start=(k == 0), stop=(k == 15)),
                            waits=([t_q] + t_wf + bw(4 + (s % 2)))
                            if k == 0 else [])
                    wrote(4 + (s % 2), mm)
                    e1 = op(DVE, lambda h, s=s: h.tensor_scalar(
                        out=ftmp[:, s * 512:(s + 1) * 512], in0=pb[4 + (s % 2)][:],
                        scalar1=vscale[:, 1:2], scalar2=None, op0=ALU.mult),
                        waits=[mm])
                    read(4 + (s % 2), e1)
                    evl.append(e1)
                t_fin = op(DVE, lambda h: h.tensor_tensor(
                    out=fyb[:], in0=fyb[:], in1=ftmp[:], op=ALU.add),
                    waits=evl + [t_yb])
                t_os = op(dC, lambda h, tb=tb: h.dma_start(
                    out=out[tb * 128:(tb + 1) * 128, :], in_=fyb[:]),
                    waits=[t_fin])
                final_st.append(t_os)
                fprev = [t_os, t_fin]
                yr_last = t_yb
            op(GP, lambda h: h.nop(), waits=final_st, inc=False)

        per_stream = {}
        for item in sched:
            per_stream.setdefault(item[0].stream, []).append(item)

        stream_eng = {"PE": "tensor", "DVE": "vector", "ACT": "scalar",
                      "POOL": "gpsimd", "SP": "sync"}

        def make_body(sname):
            items = per_stream.get(sname, [])

            def body(h):
                for eng, fn, ws, inc in items:
                    if eng.step == 16 and inc:
                        need = eng.emit in eng.fences
                        if need or eng.emit - eng.waited.get(eng.name, 0) >= 24:
                            tgt = eng.emit
                            if eng.waited.get(eng.name, 0) < tgt:
                                h.wait_ge(eng.sem, tgt * 16)
                                eng.waited[eng.name] = tgt
                        if eng.name == "dS":
                            for oth in (dC, dS):
                                if eng.waited.get("x" + oth.name, 0) < oth.emit:
                                    h.wait_ge(oth.sem, oth.emit * 16)
                                    eng.waited["x" + oth.name] = oth.emit
                        elif eng.name == "dC":
                            if eng.waited.get("xdS", 0) < dS.emit:
                                h.wait_ge(dS.sem, dS.emit * 16)
                                eng.waited["xdS"] = dS.emit
                    for w in ws:
                        pe, pc = w
                        if pc <= 0:
                            continue
                        if pe is eng and pe.step != 16:
                            if eng.drained_at < pc:
                                h.drain()
                                eng.drained_at = eng.emit
                        else:
                            if eng.waited.get(pe.name, 0) < pc:
                                h.wait_ge(pe.sem, pc * pe.step)
                                eng.waited[pe.name] = pc
                    instr = fn(h)
                    if inc:
                        eng.emit += 1
                        instr.then_inc(eng.sem, eng.step)
            return body

        block = E(nc.Block())
        for sname, attr in stream_eng.items():
            getattr(block, attr)(make_body(sname))

    return nc


def _prep(inputs):
    x = np.ascontiguousarray(np.asarray(inputs["L0_attn_x"], np.float32))
    W_qkv = np.asarray(inputs["L0_attn_W_qkv"], np.float32)
    W_z = np.asarray(inputs["L0_attn_W_z"], np.float32)
    W_b = np.asarray(inputs["L0_attn_W_b"], np.float32)
    W_a = np.asarray(inputs["L0_attn_W_a"], np.float32)
    conv_w = np.asarray(inputs["L0_attn_conv_w"], np.float32)
    dt_bias = np.asarray(inputs["L0_attn_dt_bias"], np.float32)
    A_log = np.asarray(inputs["L0_attn_A_log"], np.float32)
    norm_w = np.asarray(inputs["L0_attn_norm_w"], np.float32)
    W_out = np.asarray(inputs["L0_attn_W_out"], np.float32)
    Wf = np.asarray(inputs["L0_ffn_weight"], np.float32)
    wfs = np.asarray(inputs["L0_ffn_weight_scale"], np.float32).reshape(1, 1)
    fbias = np.asarray(inputs["L0_ffn_bias"], np.float32).reshape(1, -1)
    T = x.shape[1]
    TC = T // 4

    ident = np.eye(128, dtype=np.float32)
    identb = ident.astype(ml_dtypes.bfloat16)
    negIb = (-1e30 * np.eye(128)).astype(ml_dtypes.bfloat16)
    si = np.arange(128)[:, None]
    ti = np.arange(128)[None, :]
    m1 = (si >= ti).astype(np.float32)
    m2 = (si > ti).astype(np.float32)
    m3 = (ti >= si).astype(np.float32)
    maskp = np.concatenate([m1, m2, m3], axis=1).astype(ml_dtypes.bfloat16)
    onesf = np.ones((2, 128), np.float32)
    onesb = np.ones((2, 128), ml_dtypes.bfloat16)
    hsel = np.zeros((128, 4, 8), ml_dtypes.bfloat16)
    for hh in range(4):
        hsel[:, hh, 2 * hh] = 1
        hsel[:, hh, 2 * hh + 1] = 1
    psel = np.zeros((128, 8, 8), ml_dtypes.bfloat16)
    for pi in range(8):
        psel[:, pi, pi] = 1

    per_core = []
    for c in range(8):
        b = c // 4
        hg = c % 4
        heads = list(range(4 * hg, 4 * hg + 4))
        pair_rows = [2 * h + p for h in heads for p in range(2)]
        rows = []
        for h in heads:
            rows += list(range(2048 + h * 128, 2048 + (h + 1) * 128))
            rows += list(range(h * 128, (h + 1) * 128))
        vrows = []
        for pi_ in pair_rows:
            vrows += list(range(4096 + pi_ * 128, 4096 + (pi_ + 1) * 128))
        rows += vrows
        zrows = [r - 4096 for r in vrows]
        WP = np.concatenate([W_qkv[rows], W_z[zrows]], axis=0)
        wpT = np.ascontiguousarray(WP.T)
        WBA = np.concatenate([W_b[pair_rows], W_a[pair_rows]], axis=0)
        wbaT = np.ascontiguousarray(WBA.T)
        cw = conv_w[rows, 0, :]
        cwr = np.ascontiguousarray(
            cw.reshape(16, 128, 4).transpose(1, 0, 2))
        bavec = dt_bias[pair_rows].reshape(8, 1).astype(np.float32)
        nega = np.zeros((8, 1), np.float32)
        nega[:, 0] = A_log[pair_rows]
        woutTs = np.ascontiguousarray(
            W_out[:, [r - 4096 for r in vrows]].T)
        per_core.append(dict(
            xa=x[b], xc=x[b][hg * TC:(hg + 1) * TC],
            wpT=wpT, wbaT=wbaT, woutTs=woutTs,
            wffnT=np.ascontiguousarray(Wf.T),
            convw=cwr, bavec=bavec, alog=nega,
            normw=norm_w.reshape(128, 1), fbias=fbias, wscal=wfs,
            identf=ident, identb=identb, negIb=negIb, maskp=maskp,
            onesf=onesf, onesb=onesb, hsel=hsel, psel=psel,
        ))
    return per_core, T, TC


_NC_CACHE = {}


def kernel(**inputs):
    per_core, T, TC = _prep(inputs)
    key = (T, False)
    if key not in _NC_CACHE:
        _NC_CACHE[key] = build(T)
    nc = _NC_CACHE[key]
    res = run_bass_kernel_spmd(nc, per_core, core_ids=list(range(8)))
    out = np.zeros((2, T, 2048), np.float32)
    for c in range(8):
        b, hg = c // 4, c % 4
        out[b, hg * TC:(hg + 1) * TC, :] = res.results[c]["out"]
    return out


# version 03
# speedup vs baseline: 1.2656x; 1.2656x over previous
import os as _os
import numpy as np
import ml_dtypes
import concourse.bass as bass
import concourse.mybir as mybir
from concourse.bass_utils import run_bass_kernel_spmd
from concourse.replica_groups import maybe_share_collective_output_space
from contextlib import ExitStack

F32 = mybir.dt.float32
BF16 = mybir.dt.bfloat16
FP8 = mybir.dt.float8e4
AF = mybir.ActivationFunctionType
ALU = mybir.AluOpType
AX = mybir.AxisListType

HID = 2048
FOUT = 3072
NS = FOUT // 512
C = 128
GRP = 4
G4 = [[0, 1, 2, 3], [4, 5, 6, 7]]
G8 = [[0, 1, 2, 3, 4, 5, 6, 7]]

MID_K = lambda h: 2 * h
MID_Q = lambda h: 2 * h + 1
MID_V = lambda pi: 8 + pi
MID_Z = lambda pi: 16 + pi


class Eng:
    def __init__(self, name, stream, step=1):
        self.name, self.stream, self.step = name, stream, step
        self.sem = None
        self.plan = 0
        self.emit = 0
        self.drained_at = 0
        self.waited = {}
        self.fences = set()


def build(T, debug=False):
    nc = bass.Bass("TRN2", target_bir_lowering=False)
    nt = T // C
    ng = T // (C * GRP)
    TC = T // 4

    dI = lambda n, s, dt=F32: nc.dram_tensor(n, s, dt, kind="ExternalInput").ap()
    xa = dI("xa", [T, HID])
    xc = dI("xc", [TC, HID])
    wpT = dI("wpT", [HID, FOUT])
    wbaT = dI("wbaT", [HID, 16])
    woutTs = dI("woutTs", [1024, HID])
    wffnT = dI("wffnT", [HID, HID])
    convwD = dI("convw", [128, 16, 4])
    bavecD = dI("bavec", [8, 1])
    alogD = dI("alog", [8, 1])
    normwD = dI("normw", [128, 1])
    fbiasD = dI("fbias", [1, HID])
    wscalD = dI("wscal", [1, 1])
    identfD = dI("identf", [128, 128])
    identbD = dI("identb", [128, 128], BF16)
    negIbD = dI("negIb", [128, 128], BF16)
    maskpD = dI("maskp", [128, 384], BF16)
    onesfD = dI("onesf", [2, 128])
    onesbD = dI("onesb", [2, 128], BF16)
    hselD = dI("hsel", [128, 4, 8], BF16)
    pselD = dI("psel", [128, 8, 8], BF16)

    out = nc.dram_tensor("out", [TC, HID], F32, kind="ExternalOutput").ap()
    dbg = {}
    if debug:
        for name, shape in [("d_cpack", [2, 8, 384]), ("d_lstrip", [1, 8, 128]),
                            ("d_tmq", [128, 4, 40]), ("d_E", [128, 384]),
                            ("d_w", [128, 256]), ("d_TA", [128, 384]),
                            ("d_qc", [128, 16, T]), ("d_xln", [128, HID]),
                            ("d_rows", [8, 6, T]), ("d_o", [128, 8, T]),
                            ("d_gated", [128, 8, T]), ("d_yres", [TC, HID]),
                            ("d_xln2", [TC, HID]), ("d_ypart", [T, HID])]:
            dbg[name] = nc.dram_tensor(name, shape, F32, kind="ExternalOutput").ap()

    gatedD = nc.dram_tensor("gatedD", [8, 128, T], BF16)
    ypartD = nc.dram_tensor("ypartD", [T, HID], BF16)
    rsoutD = nc.dram_tensor("rsoutD", [TC, HID], BF16,
                            addr_space=maybe_share_collective_output_space(
                                "ReduceScatter", G4))
    stashD = nc.dram_tensor("stashD", [TC, HID], BF16)
    yresD = nc.dram_tensor("yresD", [TC, HID], F32)
    arin = nc.dram_tensor("arin", [1, 1], F32)
    arout = nc.dram_tensor("arout", [1, 1], F32,
                           addr_space=maybe_share_collective_output_space(
                               "AllReduce", G8))

    TE = Eng("TE", "PE")
    DVE = Eng("DVE", "DVE")
    SC = Eng("SC", "ACT")
    GP = Eng("GP", "POOL")
    dG = Eng("dG", "POOL", 16)
    dS = Eng("dS", "SP", 16)
    dC = Eng("dC", "SP", 16)
    ccA = Eng("ccA", "POOL")
    ccB = Eng("ccB", "POOL")
    engs = [TE, DVE, SC, GP, dG, dS, dC, ccA, ccB]
    sched = []

    barrier_state = {"toks": None, "pending": set()}

    def barrier():
        barrier_state["toks"] = [(e, e.plan) for e in engs]
        barrier_state["pending"] = set(e.name for e in engs)

    def op(eng, fn, waits=(), inc=True):
        if eng.name in barrier_state["pending"]:
            barrier_state["pending"].discard(eng.name)
            waits = list(waits) + barrier_state["toks"]
        best = {}
        for w in waits:
            if w is None:
                continue
            pe, pc = w
            if pe.step == 16:
                pc = pe.plan
                if pe is not eng:
                    pe.fences.add(pc)
            if pc > best.get(pe.name, (None, -1))[1]:
                best[pe.name] = (pe, pc)
        ws = list(best.values())
        eng.plan += 1
        tok = (eng, eng.plan)
        sched.append((eng, fn, ws, inc))
        return tok

    bank_w = {}
    bank_r = {}

    def bw(bk):
        return [bank_w.get(bk)] + bank_r.get(bk, [])

    def wrote(bk, tok):
        bank_w[bk] = tok
        bank_r[bk] = []

    def read(bk, tok):
        bank_r.setdefault(bk, []).append(tok)

    def rd(bk):
        return [bank_w.get(bk)]

    with ExitStack() as st:
        E = st.enter_context
        for e in engs:
            e.sem = E(nc.semaphore("s" + e.name))
        identf = E(nc.sbuf_tensor("identf_s", [128, 128], F32))
        identb = E(nc.sbuf_tensor("identb_s", [128, 128], BF16))
        negIb = E(nc.sbuf_tensor("negIb_s", [128, 128], BF16))
        maskp = E(nc.sbuf_tensor("maskp_s", [128, 384], BF16))
        onesf = E(nc.sbuf_tensor("onesf_s", [2, 128], F32))
        onesb = E(nc.sbuf_tensor("onesb_s", [2, 128], BF16))
        hsel = E(nc.sbuf_tensor("hsel_s", [128, 4, 8], BF16))
        psel = E(nc.sbuf_tensor("psel_s", [128, 8, 8], BF16))
        convw = E(nc.sbuf_tensor("convw_s", [128, 16, 4], F32))
        bavec = E(nc.sbuf_tensor("bavec_s", [8, 1], F32))
        nega = E(nc.sbuf_tensor("nega_s", [8, 1], F32))
        alog = E(nc.sbuf_tensor("alog_s", [8, 1], F32))
        normw = E(nc.sbuf_tensor("normw_s", [128, 1], F32))
        zero8 = E(nc.sbuf_tensor("zero8", [8, 1], F32))
        pb = [E(nc.psum_tensor(f"pb{i}", [128, 512], F32)) for i in range(8)]

        t_consts = []
        for dst, src in [(identf, identfD), (onesf, onesfD), (convw, convwD),
                         (bavec, bavecD), (alog, alogD), (normw, normwD),
                         (identb, identbD), (negIb, negIbD), (maskp, maskpD),
                         (onesb, onesbD), (hsel, hselD), (psel, pselD)]:
            t_consts.append(op(dC, lambda h, d=dst, s=src: h.dma_start(out=d[:], in_=s)))
        t_consts.append(op(GP, lambda h: h.memset(zero8[:], 0.0)))
        t_na = op(SC, lambda h: h.activation(nega[:], alog[:], AF.Exp),
                  waits=t_consts)
        t_consts.append(op(DVE, lambda h: h.tensor_scalar(
            out=nega[:], in0=nega[:], scalar1=-1.0, scalar2=None,
            op0=ALU.mult), waits=[t_na]))

        with ExitStack() as stA:
            EA = stA.enter_context
            wp = EA(nc.sbuf_tensor("wp", [128, 16, FOUT], BF16))
            wba = EA(nc.sbuf_tensor("wba", [128, 16, 16], BF16))
            x2s = EA(nc.sbuf_tensor("x2s", [128, HID], BF16))
            sbf = EA(nc.sbuf_tensor("sbf", [8, 256], BF16))
            xfm = EA(nc.sbuf_tensor("xfm", [128, 16, 128], BF16,
                                    is_transpose_dest=True))
            qzt = EA(nc.sbuf_tensor("qzt", [128, FOUT], BF16))
            xraw = qzt[:, 0:HID]
            ring = EA(nc.sbuf_tensor("ring", [128, 24, 16 + GRP * 128], BF16,
                                     is_transpose_dest=True))
            qc = EA(nc.sbuf_tensor("qc", [128, 8, GRP * 128], BF16))
            cs = [EA(nc.sbuf_tensor(f"cs{i}", [128, GRP * 128], BF16))
                  for i in range(2)]
            batm = EA(nc.sbuf_tensor("batm", [128, 16], F32))
            stat = EA(nc.sbuf_tensor("stat", [128, 16], F32))
            rows = EA(nc.sbuf_tensor("rows", [128, GRP * 128], F32))
            r_gp = rows[0:8, :]
            fmq = rows[64:104, :]
            r_rowA = EA(nc.sbuf_tensor("r_rowA8", [8, GRP * 128], F32))
            s8b = EA(nc.sbuf_tensor("s8b", [8, GRP * 128], F32))
            r_g = EA(nc.sbuf_tensor("r_g8", [8, GRP * 128], F32))
            r_beta = EA(nc.sbuf_tensor("r_beta8", [8, GRP * 128], F32))
            r_tmp = EA(nc.sbuf_tensor("r_tmp8", [8, GRP * 128], F32))
            s8a = r_tmp
            r_tmp2 = EA(nc.sbuf_tensor("r_tmp28", [8, GRP * 128], F32))
            r_lnn = EA(nc.sbuf_tensor("r_lnn8", [8, GRP * 128], F32))
            r_lam = EA(nc.sbuf_tensor("r_lam8", [8, GRP * 128], F32))

            bpre = EA(nc.sbuf_tensor("bpre", [8, GRP * 128], F32))
            apre = EA(nc.sbuf_tensor("apre", [8, GRP * 128], F32))
            tmq = EA(nc.sbuf_tensor("tmq", [128, GRP, 40], F32))
            hc3 = EA(nc.sbuf_tensor("hc3", [8, 3, 128], BF16))
            lc3 = EA(nc.sbuf_tensor("lc3", [8, 3, 128], BF16))
            cpack = EA(nc.sbuf_tensor("cpack", [2, 8, 384], BF16))
            lstrip = EA(nc.sbuf_tensor("lstrip", [1, 8, 128], BF16))
            rstrip = EA(nc.sbuf_tensor("rstrip", [1, 8, 128], BF16))
            sE12 = [EA(nc.sbuf_tensor(f"sE12_{i}", [128, 256], BF16)) for i in range(2)]
            sE3 = [EA(nc.sbuf_tensor(f"sE3_{i}", [128, 128], BF16)) for i in range(2)]
            sTp = [EA(nc.sbuf_tensor(f"sTp_{i}", [128, 128], BF16)) for i in range(2)]
            sTpT = [EA(nc.sbuf_tensor(f"sTpT_{i}", [128, 128], BF16)) for i in range(2)]
            sAp = [EA(nc.sbuf_tensor(f"sAp_{i}", [128, 128], BF16)) for i in range(2)]
            sw = [[EA(nc.sbuf_tensor(f"sw{i}_{l}", [128, 256], BF16))
                   for i in range(2)] for l in range(2)]
            sqst = [[EA(nc.sbuf_tensor(f"sqst{i}_{l}", [128, 128], BF16))
                     for i in range(2)] for l in range(2)]
            sqts = [[EA(nc.sbuf_tensor(f"sqts{i}_{l}", [128, 128], BF16))
                     for i in range(2)] for l in range(2)]
            sKb = [EA(nc.sbuf_tensor(f"sKb_{i}", [128, 128], BF16)) for i in range(2)]
            sQt = [EA(nc.sbuf_tensor(f"sQt_{i}", [128, 128], BF16)) for i in range(2)]
            sRf = [EA(nc.sbuf_tensor(f"sRf_{i}", [128, 128], BF16)) for i in range(2)]
            sMT = [EA(nc.sbuf_tensor(f"sMT_{i}", [128, 128], BF16)) for i in range(2)]
            sS32 = EA(nc.sbuf_tensor("sS32", [128, 8, 128], F32))
            sSb = EA(nc.sbuf_tensor("sSb", [128, 8, 128], BF16))
            so = EA(nc.sbuf_tensor("so", [128, 8, 128], BF16))
            so2 = EA(nc.sbuf_tensor("so2", [128, 8, 128], BF16))
            rrow = EA(nc.sbuf_tensor("rrow", [8, 128], F32))
            gtmp = EA(nc.sbuf_tensor("gtmp", [128, 128], BF16))
            gbuf = EA(nc.sbuf_tensor("gbuf", [128, 8, 128], BF16))

            t_wp = []
            for k in range(16):
                t_wp.append(op(dG, lambda h, k=k: h.dma_start(
                    out=wp[:, k, :], in_=wpT[k * 128:(k + 1) * 128, :])))
                t_wp.append(op(dG, lambda h, k=k: h.dma_start(
                    out=wba[:, k, :], in_=wbaT[k * 128:(k + 1) * 128, :])))
            t_s0 = op(GP, lambda h: h.memset(sS32[:], 0.0))
            t_s0b = op(GP, lambda h: h.memset(sSb[:], 0.0))
            t_r0 = op(GP, lambda h: h.memset(ring[:, :, 0:16], 0.0))
            allw = t_consts + t_wp + [t_s0, t_s0b, t_r0]

            st_w = [t_s0b] * 8
            st_r = [t_s0b] * 8
            prev = {"tr1": None, "tr2": None, "mm": None, "evq": None,
                    "xcons": None, "gate_done": [], "rows_done": [],
                    "carry": None, "conv_done": [], "strip_rd": [],
                    "gstore": []}
            gst_all = []

            for g in range(ng):
                tr2_toks, bar_toks = [], []
                for tt in range(GRP):
                    ti = g * GRP + tt
                    t0 = ti * C
                    t_x = op(dG, lambda h, t0=t0: h.dma_start(
                        out=xraw, in_=xa[t0:t0 + 128, :]),
                        waits=[prev["xcons"], prev["tr2"]])
                    t_sq = op(DVE, lambda h: h.scalar_tensor_tensor(
                        out=x2s[:], in0=xraw[:], scalar=1.0, in1=xraw[:],
                        op0=ALU.mult, op1=ALU.mult, accum_out=stat[:, 0:1]),
                        waits=[t_x] + prev["rows_done"])
                    t_sum = op(DVE, lambda h: h.tensor_reduce(
                        stat[:, 1:2], xraw, axis=AX.X, op=ALU.add),
                        waits=[t_x])
                    t_m = op(DVE, lambda h: h.tensor_scalar(
                        out=stat[:, 2:3], in0=stat[:, 1:2], scalar1=1.0 / HID,
                        scalar2=None, op0=ALU.mult), waits=[t_sum])
                    t_mq = op(DVE, lambda h: h.tensor_scalar(
                        out=stat[:, 3:4], in0=stat[:, 0:1], scalar1=1.0 / HID,
                        scalar2=None, op0=ALU.mult), waits=[t_sq])
                    t_m2 = op(DVE, lambda h: h.tensor_tensor(
                        out=stat[:, 4:5], in0=stat[:, 2:3], in1=stat[:, 2:3],
                        op=ALU.mult), waits=[t_m])
                    t_var = op(DVE, lambda h: h.scalar_tensor_tensor(
                        out=stat[:, 5:6], in0=stat[:, 4:5], scalar=-1.0,
                        in1=stat[:, 3:4], op0=ALU.mult, op1=ALU.add),
                        waits=[t_m2, t_mq])
                    t_ve = op(DVE, lambda h: h.tensor_scalar(
                        out=stat[:, 6:7], in0=stat[:, 5:6], scalar1=1e-5,
                        scalar2=None, op0=ALU.add), waits=[t_var])
                    t_rc = op(DVE, lambda h: h.reciprocal(
                        stat[:, 7:8], stat[:, 6:7]), waits=[t_ve])
                    t_r = op(SC, lambda h: h.activation(
                        stat[:, 8:9], stat[:, 7:8], AF.Sqrt), waits=[t_rc])
                    t_ln = op(DVE, lambda h: h.tensor_scalar(
                        out=x2s[:], in0=xraw, scalar1=stat[:, 2:3],
                        scalar2=stat[:, 8:9], op0=ALU.subtract, op1=ALU.mult),
                        waits=[t_r, prev["tr1"]])
                    t_xcons = t_ln
                    t_tr1 = op(dS, lambda h: h.dma_start(
                        out=xfm[:], in_=x2s[:], transpose=True),
                        waits=[t_ln, prev["mm"]])
                    prev["tr1"] = t_tr1
                    prev["xcons"] = t_xcons
                    if debug and ti == 0:
                        op(dG, lambda h: h.dma_start(out=dbg["d_xln"], in_=x2s[:]),
                           waits=[t_ln])
                    ev = []
                    for half in range(2):
                        pw = []
                        for s in range(3):
                            pw += bw(s)
                        last = None
                        for k in range(16):
                            for s3 in range(3):
                                s = half * 3 + s3
                                last = op(TE, lambda h, k=k, s=s, s3=s3: h.matmul(
                                    pb[s3][:], xfm[:, k, :],
                                    wp[:, k, s * 512:(s + 1) * 512],
                                    start=(k == 0), stop=(k == 15)),
                                    waits=(allw + [t_tr1] + pw)
                                    if (k == 0 and s3 == 0) else [])
                        for s3 in range(3):
                            wrote(s3, last)
                        for s3 in range(3):
                            s = half * 3 + s3
                            if s3 % 2 == 0:
                                tk = op(SC, lambda h, s=s, s3=s3: h.activation(
                                    qzt[:, s * 512:(s + 1) * 512], pb[s3][:],
                                    AF.Copy), waits=[last, prev["tr2"]])
                            else:
                                tk = op(DVE, lambda h, s=s, s3=s3: h.tensor_copy(
                                    qzt[:, s * 512:(s + 1) * 512], pb[s3][:]),
                                    waits=[last, prev["tr2"]])
                            read(s3, tk)
                            ev.append(tk)
                    pwb = bw(0)
                    t_mm = None
                    for k in range(16):
                        t_mm = op(TE, lambda h, k=k: h.matmul(
                            pb[0][:, 0:16], xfm[:, k, :], wba[:, k, :],
                            start=(k == 0), stop=(k == 15)),
                            waits=pwb if k == 0 else [])
                    prev["mm"] = t_mm
                    wrote(0, t_mm)
                    t_bae = op(DVE, lambda h: h.tensor_copy(
                        batm[:], pb[0][:, 0:16]), waits=[t_mm])
                    read(0, t_bae)
                    t_tr2 = op(dS, lambda h, tt=tt: h.dma_start(
                        out=ring[:, :, 16 + tt * 128:16 + (tt + 1) * 128],
                        in_=qzt[:], transpose=True),
                        waits=ev + [prev["carry"]] + prev["gate_done"])
                    prev["tr2"] = t_tr2
                    tr2_toks.append(t_tr2)
                    t_bat0 = op(TE, lambda h: h.transpose(
                        pb[7][0:8, 0:128], batm[:, 0:8], identf[:]),
                        waits=[t_bae] + bw(7))
                    t_bat = op(TE, lambda h: h.transpose(
                        pb[7][0:8, 128:256], batm[:, 8:16], identf[:]))
                    wrote(7, t_bat)
                    t_bar0 = op(DVE, lambda h, tt=tt: h.tensor_copy(
                        bpre[:, tt * 128:(tt + 1) * 128], pb[7][0:8, 0:128]),
                        waits=[t_bat] + prev["rows_done"])
                    t_bar = op(DVE, lambda h, tt=tt: h.tensor_copy(
                        apre[:, tt * 128:(tt + 1) * 128], pb[7][0:8, 128:256]),
                        waits=[t_bat])
                    read(7, t_bar0)
                    read(7, t_bar)
                    bar_toks += [t_bar0, t_bar]

                W = GRP * 128
                t_rdy = tr2_toks + bar_toks
                conv_done = []
                last_cs = None
                for m in range(16):
                    inplace = m < 8
                    tprev = None
                    for j in range(4):
                        if inplace:
                            dst = (cs[0][:] if j == 0 else
                                   (cs[j % 2][:] if j < 3 else
                                    ring[:, m, 16:16 + W]))
                            src_acc = cs[(j + 1) % 2][:] if j < 3 else cs[0][:]
                        else:
                            dst = qc[:, m - 8, :]
                            src_acc = qc[:, m - 8, :]
                        if j == 0:
                            tprev = op(DVE, lambda h, m=m, dst=dst: h.tensor_scalar(
                                out=dst, in0=ring[:, m, 13:13 + W],
                                scalar1=convw[:, m, 0:1], scalar2=None,
                                op0=ALU.mult),
                                waits=t_rdy + prev["conv_done"]
                                + ([last_cs] if (inplace and last_cs) else []))
                        elif j < 3:
                            tprev = op(DVE, lambda h, m=m, j=j, dst=dst, sa_=src_acc:
                                       h.scalar_tensor_tensor(
                                out=dst, in0=ring[:, m, 13 + j:13 + j + W],
                                scalar=convw[:, m, j:j + 1], in1=sa_,
                                op0=ALU.mult, op1=ALU.add), waits=[tprev])
                        else:
                            tcar = op(DVE, lambda h, m=m: h.tensor_copy(
                                ring[:, m, 13:16], ring[:, m, 13 + W:16 + W]),
                                waits=[tprev])
                            tprev = op(DVE, lambda h, m=m, j=j, dst=dst, sa_=src_acc:
                                       h.scalar_tensor_tensor(
                                out=dst, in0=ring[:, m, 16:16 + W],
                                scalar=convw[:, m, 3:4], in1=sa_,
                                op0=ALU.mult, op1=ALU.add), waits=[tcar])
                    conv_done.append(tprev)
                    if inplace:
                        last_cs = tprev
                t_zcar = op(DVE, lambda h: h.tensor_copy(
                    ring[:, 16:24, 13:16], ring[:, 16:24, 13 + W:16 + W]),
                    waits=t_rdy)
                t_conv = conv_done
                t_carry = t_zcar
                if debug:
                    op(dG, lambda h, g=g: h.dma_start(
                        out=dbg["d_qc"][:, 0:8, g * W:(g + 1) * W],
                        in_=ring[:, 0:8, 16:16 + W]), waits=t_conv)
                    op(dG, lambda h, g=g: h.dma_start(
                        out=dbg["d_qc"][:, 8:16, g * W:(g + 1) * W], in_=qc[:]),
                        waits=t_conv)

                t_sg = op(SC, lambda h: h.activation(
                    r_beta[:], bpre[:], AF.Sigmoid), waits=bar_toks)
                t_e1 = op(SC, lambda h: h.activation(
                    r_tmp[:], apre[:], AF.Exp, bias=bavec[:]),
                    waits=bar_toks)
                t_dt = op(SC, lambda h: h.activation(
                    r_tmp[:], r_tmp[:], AF.Ln, bias=1.0), waits=[t_e1])
                t_lg = op(DVE, lambda h: h.tensor_scalar(
                    out=r_tmp2[:], in0=r_tmp[:], scalar1=nega[:], scalar2=None,
                    op0=ALU.mult), waits=[t_dt])
                g_toks = []
                for cc in range(GRP):
                    g_toks.append(op(DVE, lambda h, cc=cc: h.tensor_tensor_scan(
                        out=r_g[:, cc * 128:(cc + 1) * 128],
                        data0=r_tmp2[:, cc * 128:(cc + 1) * 128],
                        data1=r_tmp2[:, cc * 128:(cc + 1) * 128],
                        initial=zero8[:], op0=ALU.add, op1=ALU.bypass),
                        waits=[t_lg]))
                t_gpv = op(DVE, lambda h: h.tensor_tensor(
                    out=r_gp[:], in0=r_g[:], in1=r_tmp2[:], op=ALU.subtract),
                    waits=g_toks)
                t_lb = op(SC, lambda h: h.activation(
                    r_tmp[:], r_beta[:], AF.Ln), waits=[t_sg, t_lg])
                ksq = []
                for hh in range(4):
                    ksq.append(op(SC, lambda h, hh=hh: h.activation(
                        x2s[:].rearrange("p (a b) -> p a b", a=4)[:, hh, 0:W],
                        ring[:, MID_K(hh), 16:16 + W], AF.Square),
                        waits=t_conv + [prev["xcons"], prev["mm"], prev["tr1"]]))
                kn = None
                for hh in range(4):
                    kn = op(TE, lambda h, hh=hh: h.matmul(
                        pb[7][0:8, 0:W], hsel[:, hh, :],
                        x2s[:].rearrange("p (a b) -> p a b", a=4)[:, hh, 0:W],
                        start=(hh == 0), stop=(hh == 3)),
                        waits=[ksq[hh]] + (bw(7) if hh == 0 else []))
                wrote(7, kn)
                t_kn = op(DVE, lambda h: h.tensor_scalar(
                    out=r_lnn[:], in0=pb[7][0:8, 0:W], scalar1=1e-24,
                    scalar2=None, op0=ALU.max), waits=[kn])
                read(7, t_kn)
                t_lnn = op(SC, lambda h: h.activation(
                    r_lnn[:], r_lnn[:], AF.Ln), waits=[t_kn])
                t_ra0 = op(DVE, lambda h: h.tensor_tensor(
                    out=r_rowA[:], in0=r_tmp[:], in1=r_g[:], op=ALU.subtract),
                    waits=[t_lb, t_gpv])
                t_ra = op(DVE, lambda h: h.scalar_tensor_tensor(
                    out=r_rowA[:], in0=r_lnn[:], scalar=-0.5, in1=r_rowA[:],
                    op0=ALU.mult, op1=ALU.add), waits=[t_ra0, t_lnn])
                t_gph = op(DVE, lambda h: h.scalar_tensor_tensor(
                    out=r_gp[:], in0=r_lnn[:], scalar=-0.5, in1=r_gp[:],
                    op0=ALU.mult, op1=ALU.add), waits=[t_gpv, t_lnn])
                t_lam = op(SC, lambda h: h.activation(
                    r_lam[:], r_g[:], AF.Exp), waits=[t_gpv, t_lb])

                t_kt1 = op(SC, lambda h: h.activation(
                    s8a[:], r_gp[:], AF.Exp), waits=[t_gph])
                t_ktr = op(dG, lambda h: h.dma_start(
                    out=rows[64:72, :], in_=s8a[:]), waits=[t_kt1])
                t_gh = op(DVE, lambda h: h.scalar_tensor_tensor(
                    out=r_tmp2[:], in0=r_lnn[:], scalar=0.5, in1=r_g[:],
                    op0=ALU.mult, op1=ALU.add), waits=[t_lnn, t_gpv, t_lg])
                kb_toks, lc_toks = [], []
                for cc in range(GRP):
                    sl = slice(cc * 128, (cc + 1) * 128)
                    tk1 = op(SC, lambda h, sl=sl, cc=cc: h.activation(
                        s8b[:, sl], r_tmp2[:, sl], AF.Exp, scale=-1.0,
                        bias=r_g[:, cc * 128 + 127:cc * 128 + 128]),
                        waits=[t_gh] + g_toks)
                    kb_toks.append(op(DVE, lambda h, sl=sl: h.scalar_tensor_tensor(
                        out=s8b[:, sl], in0=s8b[:, sl], scalar=1.0,
                        in1=r_beta[:, sl], op0=ALU.mult, op1=ALU.mult),
                        waits=[tk1, t_sg]))
                kb_dma = op(dG, lambda h: h.dma_start(
                    out=rows[72:80, :], in_=s8b[:]), waits=kb_toks)
                kb_toks.append(kb_dma)
                for cc in range(GRP):
                    sl = slice(cc * 128, (cc + 1) * 128)
                    lc_toks.append(op(SC, lambda h, sl=sl, cc=cc: h.activation(
                        s8a[:, sl], r_g[:, sl], AF.Exp, scale=0.0,
                        bias=r_g[:, cc * 128 + 127:cc * 128 + 128]),
                        waits=[t_ktr]))
                lc_dma = op(dG, lambda h: h.dma_start(
                    out=rows[96:104, :], in_=s8a[:]), waits=lc_toks)
                lc_toks.append(lc_dma)
                t_cpA = op(dG, lambda h: h.dma_start(
                    out=rows[80:88, :], in_=r_rowA[:]), waits=[t_ra])
                t_cpG = op(dG, lambda h: h.dma_start(
                    out=rows[88:96, :], in_=r_gp[:]), waits=[t_gph])

                tmq_toks = []
                for cc in range(GRP):
                    tpre = ([t_ktr, t_cpA, t_cpG] + kb_toks + lc_toks + bw(7))
                    a = op(TE, lambda h, cc=cc: h.transpose(
                        pb[7][:, 128:168], rows[64:104, cc * 128:(cc + 1) * 128],
                        identf[64:104, 64:104]), waits=tpre)
                    wrote(7, a)
                    b = op(DVE, lambda h, cc=cc: h.tensor_copy(
                        tmq[:, cc, :], pb[7][:, 128:168]), waits=[a])
                    read(7, b)
                    tmq_toks.append(b)
                rows_done = [t_ra, t_cpA, t_cpG, t_gh] + kb_toks + lc_toks + tmq_toks
                if debug:
                    dtk = []
                    for qi, src in enumerate([r_g, r_gp, r_rowA, r_beta, r_lnn, r_lam]):
                        dtk.append(op(dC, lambda h, qi=qi, src=src, g=g: h.dma_start(
                            out=dbg["d_rows"][:, qi, g * W:(g + 1) * W], in_=src[:]),
                            waits=rows_done))
                    rows_done = rows_done + dtk
                zn_toks = []
                for pi in range(8):
                    a = op(SC, lambda h, pi=pi: h.activation(
                        ring[:, MID_Z(pi), 16:16 + W], ring[:, MID_Z(pi), 16:16 + W],
                        AF.Silu), waits=t_rdy)
                    zn_toks.append(op(DVE, lambda h, pi=pi: h.tensor_scalar(
                        out=ring[:, MID_Z(pi), 16:16 + W],
                        in0=ring[:, MID_Z(pi), 16:16 + W],
                        scalar1=normw[:], scalar2=None, op0=ALU.mult), waits=[a]))

                K_NOSCAN = _os.environ.get("K_NOSCAN", "0") == "1"
                K_LVL = int(_os.environ.get("K_LVL", "8"))
                gate_done_all = []
                strip_rd = list(prev["strip_rd"])
                for cc in range(GRP if not K_NOSCAN else 0):
                    csl = slice(cc * 128, (cc + 1) * 128)
                    rsl = slice(16 + cc * 128, 16 + (cc + 1) * 128)
                    strips = []
                    hl = []
                    for qi, srcq in enumerate([r_gp, r_g, r_rowA]):
                        a = op(DVE, lambda h, qi=qi, srcq=srcq, csl=csl: h.tensor_copy(
                            hc3[:, qi, :], srcq[:, csl]),
                            waits=[t_ra, t_gh, t_gph] + prev["strip_rd"])
                        b = op(DVE, lambda h, qi=qi, srcq=srcq, csl=csl: h.tensor_tensor(
                            out=apre[:, 0:128], in0=srcq[:, csl], in1=hc3[:, qi, :],
                            op=ALU.subtract), waits=[a])
                        c2 = op(DVE, lambda h, qi=qi: h.tensor_copy(
                            lc3[:, qi, :], apre[:, 0:128]), waits=[b])
                        hl += [a, c2]
                    for hilo, srcb in [(0, hc3), (1, lc3)]:
                        strips.append(op(dG, lambda h, hilo=hilo, srcb=srcb:
                                         h.dma_start(
                            out=cpack[hilo:hilo + 1, :, :].rearrange(
                                "o p (q t) -> o p q t", q=3),
                            in_=srcb[:, :, :]),
                            waits=hl + prev["strip_rd"]))
                    t_lamb = op(DVE, lambda h, csl=csl: h.tensor_copy(
                        sbf[:, 0:128], r_lam[:, csl]),
                        waits=[t_lam] + prev["strip_rd"])
                    strips.append(op(dG, lambda h: h.dma_start(
                        out=lstrip[:], in_=sbf[:, 0:128]),
                        waits=[t_lamb]))
                    allow = strips + tmq_toks + t_conv
                    oe_l, oq_l = {}, {}
                    strip_rd = []
                    if debug and g == 0 and cc == 0:
                        op(dG, lambda h: h.dma_start(out=dbg["d_cpack"],
                                                     in_=cpack[:]), waits=strips)
                        op(dG, lambda h: h.dma_start(out=dbg["d_lstrip"],
                                                     in_=lstrip[:]), waits=strips)
                        op(dG, lambda h: h.dma_start(out=dbg["d_tmq"],
                                                     in_=tmq[:]), waits=tmq_toks)
                    for hh in range(4):
                        kk = op(TE, lambda h, hh=hh, rsl=rsl: h.matmul(
                            pb[7][:, 0:256], ring[:, MID_K(hh), rsl],
                            ring[:, 2 * hh:2 * hh + 2, rsl],
                            start=True, stop=True), waits=allow + bw(7))
                        wrote(7, kk)
                        ktp = op(TE, lambda h, hh=hh, rsl=rsl: h.transpose(
                            pb[7][:, 256:320].bitcast(BF16), ring[:, MID_K(hh), rsl],
                            identb[:]), waits=t_conv)
                        wrote(7, ktp)
                        lane_tok = {}
                        for lane in range(2):
                            pi = 2 * hh + lane
                            bX = 3 + 2 * lane
                            bA = 4 + 2 * lane
                            xm = op(TE, lambda h, bX=bX: h.matmul(
                                pb[bX][:, 0:384], negIb[:], maskp[:],
                                start=True, stop=False), waits=bw(bX) + allw)
                            xm = op(TE, lambda h, pi=pi, bX=bX: h.matmul(
                                pb[bX][:, 0:384], onesb[:], cpack[:, pi, :],
                                start=False, stop=True), waits=strips)
                            strip_rd.append(xm)
                            wrote(bX, xm)
                            e12 = op(SC, lambda h, pi=pi, cc=cc, bX=bX, lane=lane:
                                     h.activation(
                                sE12[lane][:], pb[bX][:, 0:256], AF.Exp,
                                bias=tmq[:, cc, 16 + pi:17 + pi]), waits=[xm])
                            e3 = op(SC, lambda h, pi=pi, cc=cc, bX=bX, lane=lane:
                                    h.activation(
                                sE3[lane][:], pb[bX][:, 256:384], AF.Exp,
                                bias=tmq[:, cc, 24 + pi:25 + pi]), waits=[xm])
                            read(bX, e12)
                            read(bX, e3)
                            if debug and g == 0 and cc == 0 and pi == 0:
                                op(dG, lambda h: h.dma_start(
                                    out=dbg["d_E"][:, 0:256], in_=sE12[0][:]),
                                    waits=[e12])
                                op(dG, lambda h: h.dma_start(
                                    out=dbg["d_E"][:, 256:384], in_=sE3[0][:]),
                                    waits=[e3])
                            tp = op(DVE, lambda h, lane=lane: h.scalar_tensor_tensor(
                                out=sTp[lane][:], in0=sE12[lane][:, 0:128],
                                scalar=-1.0, in1=pb[7][:, 0:128],
                                op0=ALU.mult, op1=ALU.mult), waits=[e12, kk])
                            ap_ = op(DVE, lambda h, lane=lane: h.tensor_tensor(
                                out=sAp[lane][:], in0=sE12[lane][:, 128:256],
                                in1=pb[7][:, 128:256], op=ALU.mult),
                                waits=[e12, kk])
                            tpt = op(DVE, lambda h, lane=lane: h.scalar_tensor_tensor(
                                out=sTpT[lane][:], in0=sE3[lane][:], scalar=-1.0,
                                in1=pb[7][:, 0:128], op0=ALU.mult, op1=ALU.mult),
                                waits=[e3, kk])
                            for tkx in (tp, ap_, tpt):
                                read(7, tkx)
                            vt = op(TE, lambda h, pi=pi, csl=csl, bX=bX: h.transpose(
                                pb[bX][:, 0:64].bitcast(BF16), qc[:, pi, csl],
                                identb[:]), waits=bw(bX) + t_conv)
                            wrote(bX, vt)
                            uv = op(DVE, lambda h, lane=lane, bX=bX: h.tensor_copy(
                                sw[lane][0][:, 0:128], pb[bX][:, 0:64].bitcast(BF16)),
                                waits=[vt])
                            read(bX, uv)
                            ukt = op(DVE, lambda h, pi=pi, cc=cc, lane=lane:
                                     h.tensor_scalar(
                                out=sw[lane][0][:, 128:256],
                                in0=pb[7][:, 256:320].bitcast(BF16),
                                scalar1=tmq[:, cc, pi:pi + 1], scalar2=None,
                                op0=ALU.mult), waits=[ktp] + tmq_toks)
                            read(7, ukt)
                            kbar = op(DVE, lambda h, pi=pi, cc=cc, lane=lane:
                                      h.tensor_scalar(
                                out=sKb[lane][:], in0=pb[7][:, 256:320].bitcast(BF16),
                                scalar1=tmq[:, cc, 8 + pi:9 + pi], scalar2=None,
                                op0=ALU.mult), waits=[ktp])
                            read(7, kbar)
                            lane_tok[lane] = dict(tp=tp, tpt=tpt, ap=ap_, uv=uv,
                                                  ukt=ukt, kbar=kbar)
                        wbuf = [0, 0]
                        pst = [sTp[0], sTp[1]]
                        pts = [sTpT[0], sTpT[1]]
                        ptok = [[lane_tok[0]["tp"], lane_tok[0]["tpt"]],
                                [lane_tok[1]["tp"], lane_tok[1]["tpt"]]]
                        wcur = [[lane_tok[0]["uv"], lane_tok[0]["ukt"]],
                                [lane_tok[1]["uv"], lane_tok[1]["ukt"]]]
                        for j in range(7):
                            for lane in range(2):
                                bA = 4 + 2 * lane
                                am = op(TE, lambda h, lane=lane, bA=bA,
                                        p_=pst[lane], wb=wbuf[lane]: h.matmul(
                                    pb[bA][:, 0:256], p_[:], sw[lane][wb][:],
                                    start=True, stop=True),
                                    waits=wcur[lane] + ptok[lane] + bw(bA))
                                wrote(bA, am)
                                wn = op(DVE, lambda h, lane=lane, bA=bA,
                                        wb=wbuf[lane]: h.tensor_tensor(
                                    out=sw[lane][1 - wb][:], in0=pb[bA][:, 0:256],
                                    in1=sw[lane][wb][:], op=ALU.add), waits=[am])
                                read(bA, wn)
                                wcur[lane] = [wn]
                                wbuf[lane] = 1 - wbuf[lane]
                            if j < 6:
                                for lane in range(2):
                                    bA = 4 + 2 * lane
                                    q1 = op(TE, lambda h, p_=pst[lane],
                                            q_=pts[lane], bA=bA: h.matmul(
                                        pb[bA][:, 256:384], p_[:], q_[:],
                                        start=True, stop=True),
                                        waits=ptok[lane] + bw(bA))
                                    q2 = op(TE, lambda h, p_=pst[lane],
                                            q_=pts[lane], bA=bA: h.matmul(
                                        pb[bA][:, 384:512], q_[:], p_[:],
                                        start=True, stop=True))
                                    wrote(bA, q2)
                                    nts = op(SC, lambda h, j=j, lane=lane, bA=bA:
                                             h.activation(
                                        sqts[lane][j % 2][:], pb[bA][:, 256:384],
                                        AF.Copy), waits=[q2])
                                    nst = op(SC, lambda h, j=j, lane=lane, bA=bA:
                                             h.activation(
                                        sqst[lane][j % 2][:], pb[bA][:, 384:512],
                                        AF.Copy), waits=[q2])
                                    read(bA, nts)
                                    read(bA, nst)
                                    pst[lane] = sqst[lane][j % 2]
                                    pts[lane] = sqts[lane][j % 2]
                                    ptok[lane] = [nst, nts]
                        for lane in range(2):
                            pi = 2 * hh + lane
                            bX = 3 + 2 * lane
                            bA = 4 + 2 * lane
                            lt0 = lane_tok[lane]
                            wfin = wcur[lane][0]
                            wf = sw[lane][wbuf[lane]]
                            if debug and g == 0 and cc == 0 and pi == 0:
                                op(dG, lambda h, wf=wf: h.dma_start(
                                    out=dbg["d_w"], in_=wf[:]), waits=[wfin])
                                op(dG, lambda h: h.dma_start(
                                    out=dbg["d_TA"][:, 0:128], in_=sTp[0][:]),
                                    waits=[lt0["tp"]])
                                op(dG, lambda h: h.dma_start(
                                    out=dbg["d_TA"][:, 128:256], in_=sAp[0][:]),
                                    waits=[lt0["ap"]])
                                op(dG, lambda h: h.dma_start(
                                    out=dbg["d_TA"][:, 256:384], in_=sTpT[0][:]),
                                    waits=[lt0["tpt"]])
                            m1 = op(TE, lambda h, wf=wf, bA=bA, lane=lane: h.matmul(
                                pb[bA][:, 0:128], wf[:, 128:256], sAp[lane][:],
                                start=True, stop=True),
                                waits=[wfin, lt0["ap"]] + bw(bA))
                            wrote(bA, m1)
                            lt = op(TE, lambda h, pi=pi, bX=bX: h.matmul(
                                pb[bX][:, 128:256], onesb[0:1, :],
                                lstrip[:, pi, :], start=True, stop=True),
                                waits=strips + bw(bX))
                            wrote(bX, lt)
                            strip_rd.append(lt)
                            qt = op(DVE, lambda h, hh=hh, rsl=rsl, bX=bX, lane=lane:
                                    h.tensor_tensor(
                                out=sQt[lane][:], in0=ring[:, MID_Q(hh), rsl],
                                in1=pb[bX][:, 128:256], op=ALU.mult),
                                waits=[lt] + t_conv)
                            read(bX, qt)
                            rf = op(DVE, lambda h, lane=lane, bA=bA: h.tensor_tensor(
                                out=sRf[lane][:], in0=sQt[lane][:],
                                in1=pb[bA][:, 0:128], op=ALU.subtract),
                                waits=[qt, m1])
                            read(bA, rf)
                            o1 = op(TE, lambda h, pi=pi, lane=lane: h.matmul(
                                pb[7][:, 320:448], sSb[:, pi, :], sRf[lane][:],
                                start=True, stop=False),
                                waits=[rf, st_w[pi]] + bw(7))
                            o2m = op(TE, lambda h, wf=wf, lane=lane: h.matmul(
                                pb[7][:, 320:448], wf[:, 0:128], sAp[lane][:],
                                start=False, stop=True))
                            wrote(7, o2m)
                            st_r[pi] = o1
                            oe = op(DVE, lambda h, pi=pi: h.tensor_copy(
                                so[:, pi, :], pb[7][:, 320:448]), waits=[o2m])
                            read(7, oe)
                            oq = op(SC, lambda h, pi=pi: h.activation(
                                so2[:, pi, :], so[:, pi, :], AF.Square),
                                waits=[oe])
                            oe_l[pi], oq_l[pi] = oe, oq
                            mt = op(TE, lambda h, wf=wf, bX=bX, lane=lane: h.matmul(
                                pb[bX][:, 256:384], wf[:, 128:256], sKb[lane][:],
                                start=True, stop=True),
                                waits=[wfin, lt0["kbar"]] + bw(bX))
                            wrote(bX, mt)
                            mte = op(DVE, lambda h, lane=lane, bX=bX: h.tensor_scalar(
                                out=sMT[lane][:], in0=pb[bX][:, 256:384],
                                scalar1=-1.0, scalar2=None, op0=ALU.mult),
                                waits=[mt])
                            read(bX, mte)
                            s1 = op(TE, lambda h, pi=pi, lane=lane, bA=bA: h.matmul(
                                pb[bA][:, 128:256], sMT[lane][:], sSb[:, pi, :],
                                start=True, stop=False),
                                waits=[mte, st_w[pi], st_r[pi]] + bw(bA))
                            s2 = op(TE, lambda h, wf=wf, lane=lane, bA=bA: h.matmul(
                                pb[bA][:, 128:256], sKb[lane][:], wf[:, 0:128],
                                start=False, stop=True))
                            wrote(bA, s2)
                            su = op(DVE, lambda h, pi=pi, cc=cc, bA=bA:
                                    h.scalar_tensor_tensor(
                                out=sS32[:, pi, :], in0=sS32[:, pi, :],
                                scalar=tmq[:, cc, 32 + pi:33 + pi],
                                in1=pb[bA][:, 128:256], op0=ALU.mult, op1=ALU.add),
                                waits=[s2, st_r[pi]])
                            read(bA, su)
                            sb2 = op(DVE, lambda h, pi=pi: h.tensor_copy(
                                sSb[:, pi, :], sS32[:, pi, :]),
                                waits=[su, st_r[pi]])
                            st_w[pi] = sb2
                    rm = None
                    for pi in range(8):
                        rm = op(TE, lambda h, pi=pi: h.matmul(
                            pb[7][0:8, 320:448], psel[:, pi, :], so2[:, pi, :],
                            start=(pi == 0), stop=(pi == 7)),
                            waits=[oq_l[pi]] + (bw(7) if pi == 0 else []))
                    wrote(7, rm)
                    rr = op(DVE, lambda h: h.tensor_scalar(
                        out=rrow[:], in0=pb[7][0:8, 320:448], scalar1=1.0 / 128,
                        scalar2=1e-6, op0=ALU.mult, op1=ALU.add), waits=[rm])
                    read(7, rr)
                    rr2 = op(DVE, lambda h: h.reciprocal(rrow[:], rrow[:]),
                             waits=[rr])
                    rr3 = op(SC, lambda h: h.activation(rrow[:], rrow[:], AF.Sqrt),
                             waits=[rr2])
                    rrb = op(DVE, lambda h: h.tensor_copy(sbf[:, 128:256], rrow[:]),
                             waits=[rr3])
                    rsd = op(dG, lambda h: h.dma_start(
                        out=rstrip[:], in_=sbf[:, 128:256]), waits=[rrb])
                    gate_done = []
                    last_g2 = None
                    for pi in range(8):
                        gbk = 3 + 2 * (pi % 2)
                        gb = op(TE, lambda h, pi=pi, gbk=gbk: h.matmul(
                            pb[gbk][:, 0:128], onesb[0:1, :], rstrip[:, pi, :],
                            start=True, stop=True), waits=[rsd] + bw(gbk))
                        wrote(gbk, gb)
                        g1 = op(DVE, lambda h, pi=pi, gbk=gbk: h.tensor_tensor(
                            out=gtmp[:], in0=so[:, pi, :], in1=pb[gbk][:, 0:128],
                            op=ALU.mult),
                            waits=[gb, oe_l[pi]] + ([last_g2] if last_g2 else []))
                        read(gbk, g1)
                        g2 = op(DVE, lambda h, pi=pi, cc=cc: h.tensor_tensor(
                            out=gbuf[:, pi, :], in0=gtmp[:],
                            in1=ring[:, MID_Z(pi), 16 + cc * 128:16 + (cc + 1) * 128],
                            op=ALU.mult),
                            waits=[g1] + zn_toks + prev["gstore"])
                        last_g2 = g2
                        gate_done.append(g2)
                    if debug:
                        for pi in range(8):
                            op(dG, lambda h, pi=pi, g=g, cc=cc: h.dma_start(
                                out=dbg["d_o"][:, pi,
                                               g * W + cc * 128:g * W + (cc + 1) * 128],
                                in_=so[:, pi, :]), waits=gate_done)
                    gstc = []
                    for pi in range(8):
                        gstc.append(op(dC, lambda h, pi=pi, g=g, cc=cc: h.dma_start(
                            out=gatedD.ap()[pi, :, g * W + cc * 128:
                                            g * W + (cc + 1) * 128],
                            in_=gbuf[:, pi, :]), waits=gate_done))
                    prev["gstore"] = gstc
                    gst_all += gstc
                    prev["gate_done"] = gate_done
                    gate_done_all += gate_done
                    prev["strip_rd"] = list(strip_rd[-24:])
                prev["rows_done"] = rows_done + strip_rd[-8:]
                prev["conv_done"] = strip_rd[-8:] + gate_done_all[-8:] + [t_carry]

        K_PHASES = _os.environ.get("K_PHASES", "ABC")
        if K_PHASES == "A":
            barrier()
            fin = op(dC, lambda h: h.dma_start(out=out[0:128, 0:256],
                                               in_=rows[:, 0:256]), waits=[])
            op(GP, lambda h: h.nop(), waits=[fin] + gst_all, inc=False)
            per_stream = {}
            for item in sched:
                per_stream.setdefault(item[0].stream, []).append(item)
            stream_eng = {"PE": "tensor", "DVE": "vector", "ACT": "scalar",
                          "POOL": "gpsimd", "SP": "sync"}

            def make_body(sname):
                items = per_stream.get(sname, [])

                def body(h):
                    for eng, fn, ws, inc in items:
                        if eng.step == 16 and inc:
                            need = eng.emit in eng.fences
                            if need or eng.emit - eng.waited.get(eng.name, 0) >= 24:
                                tgt = eng.emit
                                if eng.waited.get(eng.name, 0) < tgt:
                                    h.wait_ge(eng.sem, tgt * 16)
                                    eng.waited[eng.name] = tgt
                            if eng.name == "dS":
                                for oth in (dC, dS):
                                    if eng.waited.get("x" + oth.name, 0) < oth.emit:
                                        h.wait_ge(oth.sem, oth.emit * 16)
                                        eng.waited["x" + oth.name] = oth.emit
                            elif eng.name == "dC":
                                if eng.waited.get("xdS", 0) < dS.emit:
                                    h.wait_ge(dS.sem, dS.emit * 16)
                                    eng.waited["xdS"] = dS.emit
                        for w in ws:
                            pe, pc = w
                            if pc <= 0:
                                continue
                            if pe is eng and pe.step != 16:
                                if eng.drained_at < pc:
                                    h.drain()
                                    eng.drained_at = eng.emit
                            else:
                                if eng.waited.get(pe.name, 0) < pc:
                                    h.wait_ge(pe.sem, pc * pe.step)
                                    eng.waited[pe.name] = pc
                        instr = fn(h)
                        if inc:
                            eng.emit += 1
                            instr.then_inc(eng.sem, eng.step)
                return body

            block = E(nc.Block())
            for sname, attr in stream_eng.items():
                getattr(block, attr)(make_body(sname))
            return nc
        barrier()
        with ExitStack() as stB:
            EB = stB.enter_context
            wouts = EB(nc.sbuf_tensor("wouts", [128, 8, HID], BF16))
            gblk = EB(nc.sbuf_tensor("gblk", [128, 8, 512], BF16))
            yp = EB(nc.sbuf_tensor("yp", [128, HID], BF16))
            t_wo = []
            for k in range(8):
                t_wo.append(op(dG, lambda h, k=k: h.dma_start(
                    out=wouts[:, k, :], in_=woutTs[k * 128:(k + 1) * 128, :])))
            ypst = []
            prev_ev = []
            prev_st = []
            for gg in range(nt // GRP):
                gl = []
                for pi in range(8):
                    gl.append(op(dC, lambda h, pi=pi, gg=gg: h.dma_start(
                        out=gblk[:, pi, :],
                        in_=gatedD.ap()[pi, :, gg * 512:(gg + 1) * 512]),
                        waits=gst_all + prev_ev))
                for tb in range(GRP):
                    mm = None
                    for k in range(8):
                        for s in range(4):
                            mm = op(TE, lambda h, k=k, s=s, tb=tb: h.matmul(
                                pb[s][:], gblk[:, k, tb * 128:(tb + 1) * 128],
                                wouts[:, k, s * 512:(s + 1) * 512],
                                start=(k == 0), stop=(k == 7)),
                                waits=(gl + t_wo + bw(0) + bw(1) + bw(2) + bw(3))
                                if (k == 0 and s == 0) else [])
                    for s in range(4):
                        wrote(s, mm)
                    ev = []
                    for s in range(4):
                        if s % 2 == 0:
                            tk = op(SC, lambda h, s=s: h.activation(
                                yp[:, s * 512:(s + 1) * 512], pb[s][:], AF.Copy),
                                waits=[mm] + prev_st)
                        else:
                            tk = op(DVE, lambda h, s=s: h.tensor_copy(
                                yp[:, s * 512:(s + 1) * 512], pb[s][:]),
                                waits=[mm] + prev_st)
                        read(s, tk)
                        ev.append(tk)
                    prev_ev = ev
                    tb_g = gg * GRP + tb
                    t_st = op(dC, lambda h, tb_g=tb_g: h.dma_start(
                        out=ypartD.ap()[tb_g * 128:(tb_g + 1) * 128, :],
                        in_=yp[:]), waits=ev)
                    ypst.append(t_st)
                    prev_st = [t_st]
                    if debug:
                        op(dG, lambda h, tb_g=tb_g: h.dma_start(
                            out=dbg["d_ypart"][tb_g * 128:(tb_g + 1) * 128, :],
                            in_=yp[:]), waits=ev)
            t_rs = op(ccA, lambda h: h.collective_compute(
                "ReduceScatter", ALU.add, replica_groups=G4,
                ins=[ypartD.ap().opt()], outs=[rsoutD.ap().opt()]),
                waits=ypst)

        barrier()
        with ExitStack() as stC:
            EC = stC.enter_context
            rst = EC(nc.sbuf_tensor("rst", [128, HID], BF16))
            xct = EC(nc.sbuf_tensor("xct", [128, HID], F32))
            yrt = EC(nc.sbuf_tensor("yrt", [128, HID], F32))
            xl2 = EC(nc.sbuf_tensor("xl2", [128, HID], F32))
            xl2b = EC(nc.sbuf_tensor("xl2b", [128, HID], BF16))
            statc = EC(nc.sbuf_tensor("statc", [128, 16], F32))
            mxv = EC(nc.sbuf_tensor("mxv", [128, 2], F32))
            sc1 = EC(nc.sbuf_tensor("sc1", [1, 8], F32))
            vscale = EC(nc.sbuf_tensor("vscale", [128, 2], F32))
            wff = EC(nc.sbuf_tensor("wff", [128, 16, HID], BF16))
            xq8 = EC(nc.sbuf_tensor("xq8", [128, 16, 128], FP8))
            xl2r = EC(nc.sbuf_tensor("xl2r", [128, HID], BF16))
            xfmC = EC(nc.sbuf_tensor("xfmC", [128, 16, 128], BF16,
                                     is_transpose_dest=True))
            fbs = EC(nc.sbuf_tensor("fbs", [1, HID], F32))
            bbc = EC(nc.sbuf_tensor("bbc", [128, HID], F32))
            ftmp = EC(nc.sbuf_tensor("ftmp", [128, HID], F32))
            fyb = EC(nc.sbuf_tensor("fyb", [128, HID], F32))

            t_mx0 = op(GP, lambda h: h.memset(mxv[:], 0.0))
            ln2_last = None
            sq_last = None
            pst = []
            for tb in range(TC // 128):
                t_rl = op(dC, lambda h, tb=tb: h.dma_start(
                    out=rst[:], in_=rsoutD.ap()[tb * 128:(tb + 1) * 128, :]),
                    waits=[t_rs] + pst + ([ln2_last] if ln2_last else []))
                t_xl = op(dC, lambda h, tb=tb: h.dma_start(
                    out=xct[:], in_=xc[tb * 128:(tb + 1) * 128, :]),
                    waits=pst)
                t_yr = op(DVE, lambda h: h.tensor_tensor(
                    out=yrt[:], in0=xct[:], in1=rst[:], op=ALU.add),
                    waits=[t_rl, t_xl] + ([ln2_last] if ln2_last else []))
                t_ys = op(dC, lambda h, tb=tb: h.dma_start(
                    out=yresD.ap()[tb * 128:(tb + 1) * 128, :], in_=yrt[:]),
                    waits=[t_yr])
                if debug:
                    op(dC, lambda h, tb=tb: h.dma_start(
                        out=dbg["d_yres"][tb * 128:(tb + 1) * 128, :], in_=yrt[:]),
                        waits=[t_yr])
                t_sq = op(SC, lambda h: h.activation(
                    xl2[:], yrt[:], AF.Square, accum_out=statc[:, 0:1]),
                    waits=[t_yr])
                t_sum = op(DVE, lambda h: h.tensor_reduce(
                    statc[:, 1:2], yrt[:], axis=AX.X, op=ALU.add), waits=[t_yr])
                t_m = op(DVE, lambda h: h.tensor_scalar(
                    out=statc[:, 2:3], in0=statc[:, 1:2], scalar1=1.0 / HID,
                    scalar2=None, op0=ALU.mult), waits=[t_sum])
                t_mq = op(DVE, lambda h: h.tensor_scalar(
                    out=statc[:, 3:4], in0=statc[:, 0:1], scalar1=1.0 / HID,
                    scalar2=None, op0=ALU.mult), waits=[t_sq])
                t_m2 = op(DVE, lambda h: h.tensor_tensor(
                    out=statc[:, 4:5], in0=statc[:, 2:3], in1=statc[:, 2:3],
                    op=ALU.mult), waits=[t_m])
                t_var = op(DVE, lambda h: h.scalar_tensor_tensor(
                    out=statc[:, 5:6], in0=statc[:, 4:5], scalar=-1.0,
                    in1=statc[:, 3:4], op0=ALU.mult, op1=ALU.add),
                    waits=[t_m2, t_mq])
                t_ve = op(DVE, lambda h: h.tensor_scalar(
                    out=statc[:, 6:7], in0=statc[:, 5:6], scalar1=1e-5,
                    scalar2=None, op0=ALU.add), waits=[t_var])
                t_rc = op(DVE, lambda h: h.reciprocal(
                    statc[:, 7:8], statc[:, 6:7]), waits=[t_ve])
                t_r = op(SC, lambda h: h.activation(
                    statc[:, 8:9], statc[:, 7:8], AF.Sqrt), waits=[t_rc])
                t_l2 = op(DVE, lambda h: h.tensor_scalar(
                    out=xl2[:], in0=yrt[:], scalar1=statc[:, 2:3],
                    scalar2=statc[:, 8:9], op0=ALU.subtract, op1=ALU.mult),
                    waits=[t_r, t_sq])
                sq_last = t_l2
                t_l2b = op(DVE, lambda h: h.tensor_copy(xl2b[:], xl2[:]),
                           waits=[t_l2])
                t_stt = op(dC, lambda h, tb=tb: h.dma_start(
                    out=stashD.ap()[tb * 128:(tb + 1) * 128, :], in_=xl2b[:]),
                    waits=[t_l2b])
                if debug:
                    op(dC, lambda h, tb=tb: h.dma_start(
                        out=dbg["d_xln2"][tb * 128:(tb + 1) * 128, :], in_=xl2[:]),
                        waits=[t_l2])
                t_mr = op(DVE, lambda h: h.reduce_max(
                    mxv[:, 1:2], xl2[:], axis=AX.X, apply_absolute_value=True),
                    waits=[t_l2, t_mx0])
                t_mu = op(DVE, lambda h: h.tensor_tensor(
                    out=mxv[:, 0:1], in0=mxv[:, 0:1], in1=mxv[:, 1:2],
                    op=ALU.max), waits=[t_mr])
                ln2_last = t_mu
                pst = [t_ys, t_stt]
            t_mt = op(TE, lambda h: h.transpose(
                pb[2][0:2, 0:128], mxv[:, 0:2], identf[:]),
                waits=[ln2_last] + bw(2))
            wrote(2, t_mt)
            t_mv = op(DVE, lambda h: h.reduce_max(
                sc1[0:1, 0:1], pb[2][0:1, 0:128], axis=AX.X), waits=[t_mt])
            read(2, t_mv)
            t_as = op(dC, lambda h: h.dma_start(out=arin.ap(), in_=sc1[0:1, 0:1]),
                      waits=[t_mv])
            t_ar = op(ccB, lambda h: h.collective_compute(
                "AllReduce", ALU.max, replica_groups=G8,
                ins=[arin.ap().opt()], outs=[arout.ap().opt()]), waits=[t_as])
            t_al = op(dC, lambda h: h.dma_start(out=sc1[0:1, 1:2], in_=arout.ap()),
                      waits=[t_ar])
            t_s1 = op(DVE, lambda h: h.tensor_scalar(
                out=sc1[0:1, 2:3], in0=sc1[0:1, 1:2], scalar1=1.0 / 448.0,
                scalar2=1e-7, op0=ALU.mult, op1=ALU.max), waits=[t_al])
            t_s2 = op(DVE, lambda h: h.reciprocal(sc1[0:1, 3:4], sc1[0:1, 2:3]),
                      waits=[t_s1])
            t_s3 = op(DVE, lambda h: h.tensor_scalar(
                out=sc1[0:1, 4:5], in0=sc1[0:1, 3:4], scalar1=0.5, scalar2=None,
                op0=ALU.mult), waits=[t_s2])
            t_wsl = op(dC, lambda h: h.dma_start(out=sc1[0:1, 6:7], in_=wscalD))
            t_s4 = op(DVE, lambda h: h.tensor_scalar(
                out=sc1[0:1, 5:6], in0=sc1[0:1, 6:7], scalar1=2.0, scalar2=None,
                op0=ALU.mult), waits=[t_wsl])
            t_b1 = op(TE, lambda h: h.matmul(
                pb[2][:, 128:130], onesf[0:1, :], sc1[0:1, 4:6],
                start=True, stop=True), waits=[t_s3, t_s4] + bw(2))
            wrote(2, t_b1)
            t_b2 = op(DVE, lambda h: h.tensor_copy(vscale[:], pb[2][:, 128:130]),
                      waits=[t_b1])
            read(2, t_b2)
            t_wf = []
            for k in range(16):
                t_wf.append(op(dG, lambda h, k=k: h.dma_start(
                    out=wff[:, k, :], in_=wffnT[k * 128:(k + 1) * 128, :])))
            t_fb = op(dC, lambda h: h.dma_start(out=fbs[:], in_=fbiasD))
            bb = []
            for s in range(4):
                a = op(TE, lambda h, s=s: h.matmul(
                    pb[3][:, 0:512], onesf[0:1, :], fbs[0:1, s * 512:(s + 1) * 512],
                    start=True, stop=True), waits=[t_fb] + bw(3))
                wrote(3, a)
                b = op(DVE, lambda h, s=s: h.tensor_copy(
                    bbc[:, s * 512:(s + 1) * 512], pb[3][:, 0:512]), waits=[a])
                read(3, b)
                bb.append(b)
            fprev = []
            final_st = []
            yr_last = sq_last
            for tb in range(TC // 128):
                t_sl = op(dC, lambda h, tb=tb: h.dma_start(
                    out=xl2r[:], in_=stashD.ap()[tb * 128:(tb + 1) * 128, :]),
                    waits=[t_b2] + fprev + pst)
                t_t3 = op(dS, lambda h: h.dma_start(
                    out=xfmC[:], in_=xl2r[:], transpose=True),
                    waits=[t_sl] + fprev)
                t_q = op(DVE, lambda h: h.tensor_scalar(
                    out=xq8[:], in0=xfmC[:], scalar1=vscale[:, 0:1],
                    scalar2=None, op0=ALU.mult), waits=[t_t3, t_b2] + fprev)
                t_yl = op(dC, lambda h, tb=tb: h.dma_start(
                    out=yrt[:], in_=yresD.ap()[tb * 128:(tb + 1) * 128, :]),
                    waits=fprev + [yr_last, t_mu])
                t_yb = op(DVE, lambda h: h.tensor_tensor(
                    out=fyb[:], in0=yrt[:], in1=bbc[:], op=ALU.add),
                    waits=[t_yl] + bb + fprev)
                evl = []
                for s in range(4):
                    mm = None
                    for k in range(16):
                        mm = op(TE, lambda h, s=s, k=k: h.matmul(
                            pb[4 + (s % 2)][:], xq8[:, k, :],
                            wff[:, k, s * 512:(s + 1) * 512],
                            start=(k == 0), stop=(k == 15)),
                            waits=([t_q] + t_wf + bw(4 + (s % 2)))
                            if k == 0 else [])
                    wrote(4 + (s % 2), mm)
                    e1 = op(DVE, lambda h, s=s: h.tensor_scalar(
                        out=ftmp[:, s * 512:(s + 1) * 512], in0=pb[4 + (s % 2)][:],
                        scalar1=vscale[:, 1:2], scalar2=None, op0=ALU.mult),
                        waits=[mm])
                    read(4 + (s % 2), e1)
                    evl.append(e1)
                t_fin = op(DVE, lambda h: h.tensor_tensor(
                    out=fyb[:], in0=fyb[:], in1=ftmp[:], op=ALU.add),
                    waits=evl + [t_yb])
                t_os = op(dC, lambda h, tb=tb: h.dma_start(
                    out=out[tb * 128:(tb + 1) * 128, :], in_=fyb[:]),
                    waits=[t_fin])
                final_st.append(t_os)
                fprev = [t_os, t_fin]
                yr_last = t_yb
            op(GP, lambda h: h.nop(), waits=final_st, inc=False)

        per_stream = {}
        for item in sched:
            per_stream.setdefault(item[0].stream, []).append(item)

        stream_eng = {"PE": "tensor", "DVE": "vector", "ACT": "scalar",
                      "POOL": "gpsimd", "SP": "sync"}

        def make_body(sname):
            items = per_stream.get(sname, [])

            def body(h):
                for eng, fn, ws, inc in items:
                    if eng.step == 16 and inc:
                        need = eng.emit in eng.fences
                        if need or eng.emit - eng.waited.get(eng.name, 0) >= 24:
                            tgt = eng.emit
                            if eng.waited.get(eng.name, 0) < tgt:
                                h.wait_ge(eng.sem, tgt * 16)
                                eng.waited[eng.name] = tgt
                        if eng.name == "dS":
                            for oth in (dC, dS):
                                if eng.waited.get("x" + oth.name, 0) < oth.emit:
                                    h.wait_ge(oth.sem, oth.emit * 16)
                                    eng.waited["x" + oth.name] = oth.emit
                        elif eng.name == "dC":
                            if eng.waited.get("xdS", 0) < dS.emit:
                                h.wait_ge(dS.sem, dS.emit * 16)
                                eng.waited["xdS"] = dS.emit
                    for w in ws:
                        pe, pc = w
                        if pc <= 0:
                            continue
                        if pe is eng and pe.step != 16:
                            if eng.drained_at < pc:
                                h.drain()
                                eng.drained_at = eng.emit
                        else:
                            if eng.waited.get(pe.name, 0) < pc:
                                h.wait_ge(pe.sem, pc * pe.step)
                                eng.waited[pe.name] = pc
                    instr = fn(h)
                    if inc:
                        eng.emit += 1
                        instr.then_inc(eng.sem, eng.step)
            return body

        block = E(nc.Block())
        for sname, attr in stream_eng.items():
            getattr(block, attr)(make_body(sname))

    return nc


def _prep(inputs):
    x = np.ascontiguousarray(np.asarray(inputs["L0_attn_x"], np.float32))
    W_qkv = np.asarray(inputs["L0_attn_W_qkv"], np.float32)
    W_z = np.asarray(inputs["L0_attn_W_z"], np.float32)
    W_b = np.asarray(inputs["L0_attn_W_b"], np.float32)
    W_a = np.asarray(inputs["L0_attn_W_a"], np.float32)
    conv_w = np.asarray(inputs["L0_attn_conv_w"], np.float32)
    dt_bias = np.asarray(inputs["L0_attn_dt_bias"], np.float32)
    A_log = np.asarray(inputs["L0_attn_A_log"], np.float32)
    norm_w = np.asarray(inputs["L0_attn_norm_w"], np.float32)
    W_out = np.asarray(inputs["L0_attn_W_out"], np.float32)
    Wf = np.asarray(inputs["L0_ffn_weight"], np.float32)
    wfs = np.asarray(inputs["L0_ffn_weight_scale"], np.float32).reshape(1, 1)
    fbias = np.asarray(inputs["L0_ffn_bias"], np.float32).reshape(1, -1)
    T = x.shape[1]
    TC = T // 4

    ident = np.eye(128, dtype=np.float32)
    identb = ident.astype(ml_dtypes.bfloat16)
    negIb = (-1e30 * np.eye(128)).astype(ml_dtypes.bfloat16)
    si = np.arange(128)[:, None]
    ti = np.arange(128)[None, :]
    m1 = (si >= ti).astype(np.float32)
    m2 = (si > ti).astype(np.float32)
    m3 = (ti >= si).astype(np.float32)
    maskp = np.concatenate([m1, m2, m3], axis=1).astype(ml_dtypes.bfloat16)
    onesf = np.ones((2, 128), np.float32)
    onesb = np.ones((2, 128), ml_dtypes.bfloat16)
    hsel = np.zeros((128, 4, 8), ml_dtypes.bfloat16)
    for hh in range(4):
        hsel[:, hh, 2 * hh] = 1
        hsel[:, hh, 2 * hh + 1] = 1
    psel = np.zeros((128, 8, 8), ml_dtypes.bfloat16)
    for pi in range(8):
        psel[:, pi, pi] = 1

    per_core = []
    for c in range(8):
        b = c // 4
        hg = c % 4
        heads = list(range(4 * hg, 4 * hg + 4))
        pair_rows = [2 * h + p for h in heads for p in range(2)]
        rows = []
        for h in heads:
            rows += list(range(2048 + h * 128, 2048 + (h + 1) * 128))
            rows += list(range(h * 128, (h + 1) * 128))
        vrows = []
        for pi_ in pair_rows:
            vrows += list(range(4096 + pi_ * 128, 4096 + (pi_ + 1) * 128))
        rows += vrows
        zrows = [r - 4096 for r in vrows]
        WP = np.concatenate([W_qkv[rows], W_z[zrows]], axis=0)
        wpT = np.ascontiguousarray(WP.T)
        WBA = np.concatenate([W_b[pair_rows], W_a[pair_rows]], axis=0)
        wbaT = np.ascontiguousarray(WBA.T)
        cw = conv_w[rows, 0, :]
        cwr = np.ascontiguousarray(
            cw.reshape(16, 128, 4).transpose(1, 0, 2))
        bavec = dt_bias[pair_rows].reshape(8, 1).astype(np.float32)
        nega = np.zeros((8, 1), np.float32)
        nega[:, 0] = A_log[pair_rows]
        woutTs = np.ascontiguousarray(
            W_out[:, [r - 4096 for r in vrows]].T)
        per_core.append(dict(
            xa=x[b], xc=x[b][hg * TC:(hg + 1) * TC],
            wpT=wpT, wbaT=wbaT, woutTs=woutTs,
            wffnT=np.ascontiguousarray(Wf.T),
            convw=cwr, bavec=bavec, alog=nega,
            normw=norm_w.reshape(128, 1), fbias=fbias, wscal=wfs,
            identf=ident, identb=identb, negIb=negIb, maskp=maskp,
            onesf=onesf, onesb=onesb, hsel=hsel, psel=psel,
        ))
    return per_core, T, TC


_NC_CACHE = {}


def kernel(**inputs):
    per_core, T, TC = _prep(inputs)
    key = (T, False)
    if key not in _NC_CACHE:
        _NC_CACHE[key] = build(T)
    nc = _NC_CACHE[key]
    res = run_bass_kernel_spmd(nc, per_core, core_ids=list(range(8)))
    out = np.zeros((2, T, 2048), np.float32)
    for c in range(8):
        b, hg = c // 4, c % 4
        out[b, hg * TC:(hg + 1) * TC, :] = res.results[c]["out"]
    return out
